# version 01
# baseline (speedup 1.0000x reference)
from contextlib import ExitStack
import numpy as np
import concourse.bass as bass
import concourse.mybir as mybir
from concourse.bass_utils import run_bass_kernel_spmd

F32 = mybir.dt.float32
BF16 = mybir.dt.bfloat16
U8 = mybir.dt.uint8
AF = mybir.ActivationFunctionType
ALU = mybir.AluOpType
AX = mybir.AxisListType.X

T, D = 4096, 1024
NT, OT, OWN = 32, 16, 2048
H, HD, HI, HFF = 16, 64, 4, 4096
NEG_BIG = -100000.0
EPS = 1e-5
SEMS = ("dql", "dqg", "dqo", "pe", "act", "dve", "pool", "tk")


class Ctx:
    ENGS = ("tensor", "vector", "scalar", "gpsimd", "sync")

    def __init__(self, nc, es):
        self.nc = nc
        self.ops = {e: [] for e in self.ENGS}
        self.sems = {n: es.enter_context(nc.semaphore(n)) for n in SEMS}
        self.cnt = {n: 0 for n in SEMS}

    def emit(self, eng, fn, inc=None, n=1):
        if inc is not None:
            h = self.sems[inc]
            self.cnt[inc] += n
            self.ops[eng].append(lambda e, fn=fn, h=h, n=n: fn(e).then_inc(h, n))
            return self.cnt[inc]
        self.ops[eng].append(fn)
        return None

    def wait(self, eng, sem, val=None):
        h = self.sems[sem]
        v = self.cnt[sem] if val is None else val
        if v > 0:
            self.ops[eng].append(lambda e, h=h, v=v: e.wait_ge(h, v))

    def val(self, sem):
        return self.cnt[sem]

    def barrier(self):
        snap = dict(self.cnt)
        for eng in self.ENGS:
            for s, v in snap.items():
                if v > 0:
                    h = self.sems[s]
                    self.ops[eng].append(lambda e, h=h, v=v: e.wait_ge(h, v))

    def finalize(self, block):
        def runner(lst):
            def _(eng):
                for f in lst:
                    f(eng)
            return _
        for ename in self.ENGS:
            getattr(block, ename)(runner(self.ops[ename]))


def build(upto="D", debug=()):
    nc = bass.Bass("TRN2", target_bir_lowering=False)

    def din(name, shape, dtype=F32):
        return nc.dram_tensor(name, shape, dtype, kind="ExternalInput").ap()

    x_full = din("x_full", [T, D]); x_own = din("x_own", [OWN, D])
    pos_qt = din("pos_qt", [128, OT])
    wq = din("wq", [D, D]); wk = din("wk", [D, D]); wv = din("wv", [D, D])
    wo = din("wo", [D, D]); bo = din("bo", [1, D])
    wqi = din("wqi", [D, HI * HD]); wki = din("wki", [D, HD]); www = din("www", [D, HI])
    w1t = din("w1t", [D, HFF]); b1c = din("b1c", [128, 32])
    w2t = din("w2t", [HFF, D]); b2 = din("b2", [1, D])
    scalei = din("scalei", [1, 1])
    out_own = nc.dram_tensor("out_own", [OWN, D], F32, kind="ExternalOutput").ap()

    v_scr = nc.dram_tensor("v_scr", [T, H * 65], BF16, kind="Internal").ap()
    xn0_scr = nc.dram_tensor("xn0_scr", [128, D], BF16, kind="Internal").ap()
    ctx_scr = nc.dram_tensor("ctx_scr", [H, HD, OWN], BF16, kind="Internal").ap()
    p1_scr = nc.dram_tensor("p1_scr", [4, 512], BF16, kind="Internal").ap()
    e1_scr = nc.dram_tensor("e1_scr", [1, HI * D], BF16, kind="Internal").ap()
    e2_scr = nc.dram_tensor("e2_scr", [16, D], BF16, kind="Internal").ap()

    dbg = {}
    for name, shape, dtype in debug:
        dbg[name] = nc.dram_tensor("dbg_" + name, shape, dtype,
                                   kind="ExternalOutput").ap()

    es = ExitStack()
    c = Ctx(nc, es)

    def SB(stack, name, shape, dtype):
        return stack.enter_context(nc.sbuf_tensor(name, shape, dtype))

    def PS(stack, name, shape, dtype=F32):
        return stack.enter_context(nc.psum_tensor(name, shape, dtype))

    def dump(name, ap_src):
        if name in dbg:
            c.barrier()
            c.emit("gpsimd", lambda g, s=ap_src, d=dbg[name]: g.dma_start(
                out=d, in_=s), inc="dqo", n=16)

    ident_bf = SB(es, "ident_bf", [128, 128], BF16)
    ones1r = SB(es, "ones1r", [1, 128], F32)
    ones2c = SB(es, "ones2c", [128, 2], BF16)
    posq_sb = SB(es, "posq_sb", [128, OT], F32)
    b1c_sb = SB(es, "b1c_sb", [128, 32], F32)
    bias_h = SB(es, "bias_h", [128, H], F32)
    xnO = SB(es, "xnO", [128, 8 * OWN], BF16)
    o1T = SB(es, "o1T", [128, 8 * 4], BF16)
    o2raw = SB(es, "o2raw", [128, 32], BF16)
    o2TX = SB(es, "o2TX", [128, 32 * 16], BF16)
    scale1 = SB(es, "scale1", [1, 1], F32)
    scale4 = SB(es, "scale4", [4, 1], F32)
    neg1f = SB(es, "neg1f", [128, 256], F32)
    zer_bf = SB(es, "zer_bf", [1, 512], BF16)
    epsc = SB(es, "epsc", [128, 1], F32)
    negbigb = SB(es, "negbigb", [128, 256], BF16)

    def g0(g):
        g.memset(ident_bf[:], 0.0)
        g.memset(ones1r[:], 1.0)
        g.memset(ones2c[:], 0.0)
        g.memset(ones2c[0:64, 0:1], 1.0)
        g.memset(ones2c[64:128, 1:2], 1.0)
        g.memset(neg1f[:], -1.0)
        g.memset(zer_bf[:], 0.0)
        g.memset(epsc[:], EPS)
        return g.memset(negbigb[:], NEG_BIG)
    import os as _os
    _skip = _os.environ.get("DKSKIP", "")
    if "g0" not in _skip:
        c.emit("gpsimd", g0, inc="pool")
        c.wait("gpsimd", "pool")
    if "af" not in _skip:
        c.wait("gpsimd", "pool")
        c.emit("gpsimd", lambda g: g.affine_select(
            out=ident_bf[:], in_=ident_bf[:], compare_op=ALU.not_equal, fill=1.0,
            base=0, pattern=[[-1, 128]], channel_multiplier=1), inc="pool")

    if "s0" not in _skip:
        c.emit("sync", lambda sy: sy.dma_start(out=posq_sb[:], in_=pos_qt),
               inc="dql", n=16)
        c.emit("sync", lambda sy: sy.dma_start(out=b1c_sb[:], in_=b1c),
               inc="dql", n=16)
        c.emit("sync", lambda sy: sy.dma_start(out=scale1[:], in_=scalei),
               inc="dql", n=16)

    esR1 = ExitStack()
    kT = SB(esR1, "kT", [128, 8 * T], BF16)
    kiT = SB(esR1, "kiT", [128, T], BF16)

    esR2 = ExitStack()
    xnT = SB(esR2, "xnT", [128, 8 * T], BF16)
    wstg = SB(esR2, "wstg", [128, 8 * D], BF16)
    vstg = [SB(esR2, f"vstg{i}", [128, H * 65], BF16) for i in range(2)]
    xstg = [SB(esR2, f"xstg{i}", [128, D], F32) for i in range(2)]
    lncols = SB(esR2, "lncols", [128, 48], F32)
    sumsq, s1c, mcol, vcol, mm2, rstd = (lncols[:, i:i + 1]
                                         for i in (0, 8, 16, 24, 32, 40))
    xn_tok = [SB(esR2, f"xn_tok{i}", [128, D], BF16) for i in range(2)]
    psA = PS(esR2, "psA", [128, D], BF16)
    psB = PS(esR2, "psB", [128, D], BF16)
    psQ = PS(esR2, "psQ", [128, D], F32)
    psP = [PS(esR2, f"psP{i}", [128, 512], F32) for i in range(2)]
    psV = psQ
    psS = PS(esR2, "psS", [128, 512], F32)

    def ln_tiles(src_ap, ntiles, dst_sb, save0, B):
        psA, psB, psQ = B["psA"], B["psB"], B["psQ"]
        xstg, xn_tok = B["xstg"], B["xn_tok"]
        sumsq, s1c, mcol, vcol, mm2, rstd = B["cols"]
        psum = [psA, psB]
        dst3 = dst_sb.rearrange("p (cc n) -> p cc n", cc=8)
        ln_done, ev_done = {}, {}
        for tt in range(ntiles):
            buf = tt % 2
            if tt >= 2:
                c.wait("sync", "dve", ln_done[tt - 2])
            c.emit("sync", lambda sy, tt=tt, buf=buf: sy.dma_start(
                out=xstg[buf][:], in_=src_ap[tt * 128:(tt + 1) * 128, :]),
                inc="dql", n=16)
            ldq = c.val("dql")
            c.wait("scalar", "dql", ldq)
            if tt >= 1:
                c.wait("scalar", "dve", ln_done[tt - 1])
            c.emit("scalar", lambda s, buf=buf: s.activation(
                out=psQ[:], in_=xstg[buf][:], func=AF.Square), inc="act")
            a_sq = c.val("act")
            c.wait("vector", "dql", ldq)
            c.wait("vector", "act", a_sq)
            c.emit("vector", lambda v: v.reduce_sum(
                out=sumsq[:], in_=psQ[:], axis=AX), inc="dve")
            c.emit("vector", lambda v, buf=buf: v.reduce_sum(
                out=s1c[:], in_=xstg[buf][:], axis=AX), inc="dve")
            c.wait("vector", "dve")
            c.emit("vector", lambda v: v.tensor_scalar(
                out=mcol[:], in0=s1c[:], scalar1=1.0 / D, scalar2=None,
                op0=ALU.mult), inc="dve")
            c.emit("vector", lambda v: v.tensor_scalar(
                out=vcol[:], in0=sumsq[:], scalar1=1.0 / D, scalar2=None,
                op0=ALU.mult), inc="dve")
            c.wait("vector", "dve")
            c.emit("vector", lambda v: v.scalar_tensor_tensor(
                out=mm2[:], in0=mcol[:], scalar=-1.0, in1=mcol[:],
                op0=ALU.mult, op1=ALU.mult), inc="dve")
            c.wait("vector", "dve")
            c.emit("vector", lambda v: v.tensor_add(
                out=vcol[:], in0=vcol[:], in1=mm2[:]), inc="dve")
            c.wait("scalar", "dve")
            c.emit("scalar", lambda s: s.activation(
                out=rstd[:], in_=vcol[:], func=AF.Ln, bias=epsc[:], scale=1.0),
                inc="act")
            c.wait("scalar", "act")
            c.emit("scalar", lambda s: s.activation(
                out=rstd[:], in_=rstd[:], func=AF.Exp, bias=0.0, scale=-0.5), inc="act")
            c.wait("vector", "act")
            if tt >= 2:
                c.wait("vector", "pe", tr_done[tt - 2])
            c.emit("vector", lambda v, buf=buf: v.tensor_scalar(
                out=xn_tok[buf][:], in0=xstg[buf][:], scalar1=mcol[:],
                scalar2=rstd[:], op0=ALU.subtract, op1=ALU.mult), inc="dve")
            ln_done[tt] = c.val("dve")
            c.wait("tensor", "dve", ln_done[tt])
            if tt >= 2:
                c.wait("tensor", "act", ev_done[tt - 2])
            for j in range(8):
                c.emit("tensor", lambda t, buf=buf, j=j: t.transpose(
                    psum[buf][:, j * 128:(j + 1) * 128],
                    xn_tok[buf][:, j * 128:(j + 1) * 128], ident_bf[:]), inc="pe")
            tr_done[tt] = c.val("pe")
            c.wait("vector", "pe", tr_done[tt])
            c.wait("scalar", "pe", tr_done[tt])
            c.emit("scalar", lambda s, tt=tt, buf=buf: s.copy(
                out=dst3[:, :, tt * 128:(tt + 1) * 128],
                in_=psum[buf][:].rearrange("p (cc n) -> p cc n", cc=8)), inc="act")
            ev_done[tt] = c.val("act")
            c.barrier()
            if save0 and tt == 0:
                c.wait("gpsimd", "dve", ln_done[0])
                c.emit("gpsimd", lambda g: g.dma_start(
                    out=xn0_scr, in_=xn_tok[0][:]), inc="dqo", n=16)

    tr_done = {}
    c.wait("vector", "pool")
    c.wait("tensor", "pool")
    n_ln = NT
    if upto.startswith("A") and len(upto) > 1:
        n_ln = int(upto[1:])
    if upto.startswith("Bw") and len(upto) > 2:
        n_ln = int(upto[2:])
    lnB1 = dict(psA=psA, psB=psB, psQ=psQ, xstg=xstg, xn_tok=xn_tok,
                cols=(sumsq, s1c, mcol, vcol, mm2, rstd))
    ln_tiles(x_full, n_ln, xnT, save0=True, B=lnB1)
    c.barrier()
    if upto.startswith("A") and len(upto) > 1:
        dump("xnT", xnT[:, 0:2048])
        dump("xntok", xn_tok[0][:])
        dump("sumsq", sumsq[:])
        dump("s1c", s1c[:])
        dump("vcol", vcol[:])
        dump("rstd", rstd[:])
        dump("mcol", mcol[:])
        dump("mm2", mm2[:])
        dump("xstg0", xstg[0][:])
    if upto == "A":
        dump("xnT", xnT[:, 0:2048])
        dump("sumsq", sumsq[:])
        dump("s1c", s1c[:])
        dump("vcol", vcol[:])
        dump("rstd", rstd[:])
        dump("xntok", xn_tok[1][:])

    if upto.startswith("A") and len(upto) > 1:
        c.wait("sync", "dqo")
        c.barrier()
        esR2.close()
        esR1.close()
        with nc.Block() as block:
            c.finalize(block)
        es.close()
        return nc, dbg

    def load_w(dram_ap, ncols):
        for dc in range(8):
            c.emit("gpsimd", lambda g, dc=dc: g.dma_start(
                out=wstg[:, dc * D:dc * D + ncols],
                in_=dram_ap[dc * 128:(dc + 1) * 128, :]), inc="dqg", n=16)

    lvl = {"Bz": -1, "Bw": 0, "Bk": 1, "Bv": 2, "Bki": 3}.get(
        "Bw" if upto.startswith("Bw") else upto, 9)
    if upto != "A":
        if lvl >= 0:
            load_w(wk, D)
        c.wait("tensor", "dqg")
        ev = {}
        for j in range(8 if lvl >= 1 else 0):
            for piece in range(8):
                it = j * 8 + piece
                pp = psP[it % 2]
                if it >= 2:
                    c.wait("tensor", "act", ev[it - 2])
                for dc in range(8):
                    c.emit("tensor", lambda t, j=j, piece=piece, dc=dc, pp=pp: t.matmul(
                        pp[:], wstg[:, dc * D + j * 128:dc * D + (j + 1) * 128],
                        xnT[:, dc * T + piece * 512:dc * T + (piece + 1) * 512],
                        start=(dc == 0), stop=(dc == 7)), inc="pe" if dc == 7 else None)
                c.wait("scalar", "pe")
                c.emit("scalar", lambda s, j=j, piece=piece, pp=pp: s.copy(
                    out=kT[:, j * T + piece * 512:j * T + (piece + 1) * 512],
                    in_=pp[:]), inc="act")
                ev[it] = c.val("act")

        c.wait("gpsimd", "pe")
        if lvl >= 2:
         load_w(wv, D)
         c.wait("tensor", "dqg")
         for b_ in range(2):
             c.emit("gpsimd", lambda g, b_=b_: g.memset(
                 vstg[b_].rearrange("p (h e) -> p h e", e=65)[:, :, 64:65], 1.0),
                 inc="pool")
         ev_v, st_v = {}, {}
         for tt in range(NT):
             buf = tt % 2
             if tt >= 1:
                 c.wait("tensor", "act", ev_v[tt - 1])
             for dc in range(8):
                 for piece in range(2):
                     c.emit("tensor", lambda t, tt=tt, dc=dc, piece=piece: t.matmul(
                         psV[:, piece * 512:(piece + 1) * 512],
                         xnT[:, dc * T + tt * 128:dc * T + (tt + 1) * 128],
                         wstg[:, dc * D + piece * 512:dc * D + (piece + 1) * 512],
                         start=(dc == 0), stop=(dc == 7)),
                         inc="pe" if (dc == 7 and piece == 1) else None)
             c.wait("scalar", "pe")
             c.wait("scalar", "pool")
             if tt >= 2:
                 c.wait("scalar", "dqo", st_v[tt - 2])
             c.emit("scalar", lambda s, buf=buf: s.copy(
                 out=vstg[buf].rearrange("p (h e) -> p h e", e=65)[:, :, 0:64],
                 in_=psV[:].rearrange("p (h e) -> p h e", e=64)), inc="act")
             ev_v[tt] = c.val("act")
             c.wait("sync", "act", ev_v[tt])
             c.emit("sync", lambda sy, tt=tt, buf=buf: sy.dma_start(
                 out=v_scr[tt * 128:(tt + 1) * 128, :], in_=vstg[buf][:]),
                 inc="dqo", n=16)
             st_v[tt] = c.val("dqo")

        if lvl >= 3:
         c.wait("gpsimd", "pe")
         for dc in range(8):
             for hf in range(2):
                 c.emit("gpsimd", lambda g, dc=dc, hf=hf: g.dma_start(
                     out=wstg[:, dc * D + hf * 64:dc * D + hf * 64 + 64],
                     in_=wki[dc * 128:(dc + 1) * 128, :]), inc="dqg", n=16)
         c.wait("tensor", "dqg")
         ev = {}
         for piece in range(8):
             pp = psP[piece % 2]
             if piece >= 2:
                 c.wait("tensor", "act", ev[piece - 2])
             for dc in range(8):
                 c.emit("tensor", lambda t, piece=piece, dc=dc, pp=pp: t.matmul(
                     pp[:], wstg[:, dc * D:dc * D + 128],
                     xnT[:, dc * T + piece * 512:dc * T + (piece + 1) * 512],
                     start=(dc == 0), stop=(dc == 7)), inc="pe" if dc == 7 else None)
             c.wait("scalar", "pe")
             c.emit("scalar", lambda s, piece=piece, pp=pp: s.copy(
                 out=kiT[:, piece * 512:(piece + 1) * 512], in_=pp[:]), inc="act")
             ev[piece] = c.val("act")

        if lvl >= 4:
         for dc in range(8):
             c.emit("vector", lambda v, dc=dc: v.tensor_copy(
                 out=o1T[:, dc * 4:(dc + 1) * 4], in_=xnT[:, dc * T + 2:dc * T + 10:2]))
             c.emit("vector", lambda v, dc=dc: v.tensor_copy(
                 out=o2raw[:, dc * 4:(dc + 1) * 4], in_=xnT[:, dc * T + 10:dc * T + 18:2]),
                 inc="dve")
         c.emit("gpsimd", lambda g: g.memset(o2TX[:], 0.0), inc="pool")
         c.wait("vector", "pool")
         c.wait("vector", "dve")
         for hc in range(32):
             hi, dc = hc // 8, hc % 8
             for i in range(4):
                 c.emit("vector", lambda v, hc=hc, hi=hi, dc=dc, i=i: v.tensor_copy(
                     out=o2TX[:, hc * 16 + i * 4 + hi:hc * 16 + i * 4 + hi + 1],
                     in_=o2raw[:, dc * 4 + i:dc * 4 + i + 1]),
                     inc="dve" if (hc == 31 and i == 3) else None)

         e1tmp = vstg[0][0:1, 0:D]
         e2t = vstg[1][:, 0:8]
         e2w = vstg[1][0:4, 8:264]
         c.wait("gpsimd", "dqo")
         c.wait("tensor", "dql")
         c.emit("tensor", lambda t: t.matmul(
             psS[0:4, 16:17], ones1r[0:1, 0:4], scale1[:], start=True, stop=True),
             inc="pe")
         c.wait("scalar", "pe")
         c.emit("scalar", lambda s: s.copy(out=scale4[:], in_=psS[0:4, 16:17]),
                inc="act")
         for i in range(4):
             c.emit("gpsimd", lambda g, i=i: g.dma_start(
                 out=e1tmp[:], in_=xn0_scr[2 * i + 1:2 * i + 2, :]), inc="dqg", n=16)
             c.wait("vector", "dqg")
             c.wait("vector", "act")
             c.emit("vector", lambda v: v.tensor_scalar(
                 out=e1tmp[:], in0=e1tmp[:], scalar1=scale1[:], scalar2=None,
                 op0=ALU.mult), inc="dve")
             c.wait("gpsimd", "dve")
             c.emit("gpsimd", lambda g, i=i: g.dma_start(
                 out=e1_scr[0:1, i * D:(i + 1) * D], in_=e1tmp[:]), inc="dqo", n=16)
             c.wait("gpsimd", "dqo")
         for i in range(4):
             ro = 9 + 2 * i
             src = xn0_scr[ro:ro + 1, :].rearrange("a (r hi) -> (a r) hi", hi=4)
             for hb in range(2):
                 c.emit("gpsimd", lambda g, hb=hb, src=src: g.dma_start(
                     out=e2t[:, hb * 4:(hb + 1) * 4],
                     in_=src[hb * 128:(hb + 1) * 128, :]), inc="dqg", n=16)
             c.wait("tensor", "dqg")
             if i > 0:
                 c.wait("tensor", "dqo", c.val("dqo"))
             for hb in range(2):
                 c.emit("tensor", lambda t, hb=hb: t.transpose(
                     psA[0:4, hb * 128:(hb + 1) * 128], e2t[:, hb * 4:(hb + 1) * 4],
                     ident_bf[:]), inc="pe")
             c.wait("scalar", "pe")
             c.wait("scalar", "act")
             c.emit("scalar", lambda s: s.activation(
                 out=e2w[:], in_=psA[0:4, 0:256], func=AF.Copy, bias=0.0,
                 scale=scale4[:]), inc="act")
             c.wait("gpsimd", "act")
             c.emit("gpsimd", lambda g, i=i: g.dma_start(
                 out=e2_scr[4 * i:4 * i + 4, i * 256:(i + 1) * 256], in_=e2w[:]),
                 inc="dqo", n=16)
         zrow4 = e2w
         c.wait("vector", "dqo")
         c.emit("vector", lambda v: v.memset(e2w[:], 0.0), inc="dve")
         c.wait("gpsimd", "dve")
         for i in range(4):
             for oi in range(4):
                 if oi != i:
                     c.emit("gpsimd", lambda g, i=i, oi=oi: g.dma_start(
                         out=e2_scr[4 * i:4 * i + 4, oi * 256:(oi + 1) * 256],
                         in_=e2w[:]), inc="dqo", n=16)
         c.barrier()

    if upto in ("A", "B1", "Bz", "Bk", "Bv", "Bki") or upto.startswith("Bw"):
        dump("wstgd", wstg[:, 0:256])
        dump("wstgb", wstg[:, 0:256])
        if "wkraw" in dbg:
            c.barrier()
            c.emit("sync", lambda sy: sy.dma_start(
                out=xstg[0][:], in_=wk[0:128, :]), inc="dql", n=16)
            c.wait("gpsimd", "dql")
            c.emit("gpsimd", lambda g: g.dma_start(
                out=dbg["wkraw"], in_=xstg[0][:, 0:256]), inc="dqo", n=16)
        if "wkg" in dbg:
            c.barrier()
            c.emit("gpsimd", lambda g: g.dma_start(
                out=xstg[1][:], in_=wk[0:128, :]), inc="dqg", n=16)
            c.wait("gpsimd", "dqg")
            c.emit("gpsimd", lambda g: g.dma_start(
                out=dbg["wkg"], in_=xstg[1][:, 0:256]), inc="dqo", n=16)
        dump("xnTb", xnT[:, 0:2048])
        dump("kT", kT[:, 0:4096])
        dump("kiT", kiT[:, 0:4096])

    esR2.close()

    esR1b = ExitStack()
    qT = SB(esR1b, "qT", [128, 8 * OWN], BF16)
    qiT = SB(esR1b, "qiT", [128, 2 * OWN], BF16)

    if upto not in ("A",):
        esR3 = ExitStack()
        wstg2 = SB(esR3, "wstg3", [128, 8 * D], BF16)
        xstg2 = [SB(esR3, f"xstg3{i}", [128, D], F32) for i in range(2)]
        lncols3 = SB(esR3, "lncols3", [128, 48], F32)
        sumsq2, s1c2, mcol2, vcol2, mm22, rstd2 = (lncols3[:, i:i + 1]
                                             for i in (0, 8, 16, 24, 32, 40))
        xn_tok2 = [SB(esR3, f"xn_tok3{i}", [128, D], BF16) for i in range(2)]
        psA2 = PS(esR3, "psA2", [128, D], BF16)
        psB2 = PS(esR3, "psB2", [128, D], BF16)
        psQ2 = PS(esR3, "psQ2", [128, D], F32)
        psP2 = [PS(esR3, f"psP2{i}", [128, 512], F32) for i in range(2)]
        psV2 = psQ2
        psS2 = PS(esR3, "psS2", [128, 512], F32)

        tr_done.clear()
        lnB2 = dict(psA=psA2, psB=psB2, psQ=psQ2, xstg=xstg2, xn_tok=xn_tok2,
                    cols=(sumsq2, s1c2, mcol2, vcol2, mm22, rstd2))
        ln_tiles(x_own, OT, xnO, save0=False, B=lnB2)
        c.barrier()

        def load_w3(dram_ap, ncols):
            for dc in range(8):
                c.emit("gpsimd", lambda g, dc=dc: g.dma_start(
                    out=wstg2[:, dc * D:dc * D + ncols],
                    in_=dram_ap[dc * 128:(dc + 1) * 128, :]), inc="dqg", n=16)

        load_w3(wq, D)
        c.wait("tensor", "dqg")
        ev = {}
        for j in range(8):
            for piece in range(4):
                it = j * 4 + piece
                pp = psP2[it % 2]
                if it >= 2:
                    c.wait("tensor", "act", ev[it - 2])
                for dc in range(8):
                    c.emit("tensor", lambda t, j=j, piece=piece, dc=dc, pp=pp: t.matmul(
                        pp[:], wstg2[:, dc * D + j * 128:dc * D + (j + 1) * 128],
                        xnO[:, dc * OWN + piece * 512:dc * OWN + (piece + 1) * 512],
                        start=(dc == 0), stop=(dc == 7)), inc="pe" if dc == 7 else None)
                c.wait("scalar", "pe")
                c.emit("scalar", lambda s, j=j, piece=piece, pp=pp: s.copy(
                    out=qT[:, j * OWN + piece * 512:j * OWN + (piece + 1) * 512],
                    in_=pp[:]), inc="act")
                ev[it] = c.val("act")

        ksq = SB(esR3, "ksq", [128, 512], BF16)
        zpad = SB(esR3, "zpad", [128, 8], F32)
        kqmax = SB(esR3, "kqmax", [2, 32], F32)
        crow = SB(esR3, "crow", [1, 32], F32)
        c.emit("gpsimd", lambda g: g.memset(kqmax[:], 0.0), inc="pool")
        c.wait("vector", "pool")

        def sqmax(src_sb, src_T, npiece, outcol):
            pe_sq, dve_mx = {}, {}
            for j in range(8):
                for piece in range(npiece):
                    it = j * npiece + piece
                    if it >= 1:
                        c.wait("vector", "pe", pe_sq[it - 1])
                    c.emit("vector", lambda v, j=j, piece=piece: v.tensor_mul(
                        out=ksq[:],
                        in0=src_sb[:, j * src_T + piece * 512:j * src_T + (piece + 1) * 512],
                        in1=src_sb[:, j * src_T + piece * 512:j * src_T + (piece + 1) * 512]),
                        inc="dve")
                    c.wait("tensor", "dve")
                    if it >= 1:
                        c.wait("tensor", "dve", dve_mx[it - 1])
                    c.emit("tensor", lambda t: t.matmul(
                        psS2[0:2, 0:512], ones2c[:], ksq[:], start=True, stop=True),
                        inc="pe")
                    pe_sq[it] = c.val("pe")
                    c.wait("vector", "pe", pe_sq[it])
                    c.emit("vector", lambda v: v.reduce_max(
                        out=zpad[0:2, 0:1], in_=psS2[0:2, 0:512], axis=AX), inc="dve")
                    c.wait("vector", "dve")
                    c.emit("vector", lambda v, j=j: v.tensor_max(
                        out=kqmax[0:2, outcol + j:outcol + j + 1],
                        in0=kqmax[0:2, outcol + j:outcol + j + 1],
                        in1=zpad[0:2, 0:1]), inc="dve")
                    dve_mx[it] = c.val("dve")

        sqmax(kT, T, 8, 0)
        sqmax(qT, OWN, 4, 16)
        c.wait("vector", "dve")
        c.emit("vector", lambda v: v.tensor_add(
            out=kqmax[0:2, 0:16], in0=kqmax[0:2, 0:16], in1=kqmax[0:2, 16:32]),
            inc="dve")
        c.wait("vector", "dve")
        c.emit("vector", lambda v: v.tensor_scalar(
            out=kqmax[0:2, 0:16], in0=kqmax[0:2, 0:16], scalar1=1.0 / 16.0,
            scalar2=None, op0=ALU.mult), inc="dve")
        c.wait("gpsimd", "dve")
        c.emit("gpsimd", lambda g: g.dma_start(out=crow[:], in_=kqmax[0:2, 0:16]),
               inc="dqg", n=16)
        c.wait("tensor", "dqg")
        c.wait("tensor", "dve")
        for h in range(16):
            col = (h % 2) * 16 + h // 2
            c.emit("tensor", lambda t, h=h, col=col: t.matmul(
                psS2[:, h:h + 1], ones1r[:], crow[0:1, col:col + 1],
                start=True, stop=True), inc="pe" if h == 15 else None)
        c.wait("scalar", "pe")
        c.emit("scalar", lambda s: s.activation(
            out=bias_h[:], in_=psS2[:, 0:16], func=AF.Copy, bias=0.0, scale=-1.0),
            inc="act")

        c.wait("gpsimd", "pe")
        for dc in range(8):
            c.emit("gpsimd", lambda g, dc=dc: g.dma_start(
                out=wstg2[:, dc * D:dc * D + 256], in_=wqi[dc * 128:(dc + 1) * 128, :]),
                inc="dqg", n=16)
            c.emit("gpsimd", lambda g, dc=dc: g.dma_start(
                out=wstg2[:, dc * D + 256:dc * D + 260],
                in_=www[dc * 128:(dc + 1) * 128, :]), inc="dqg", n=16)
        c.wait("tensor", "dqg")
        wlog = SB(esR3, "wlog", [128, 1], F32)
        w01 = SB(esR3, "w01", [128, 4], F32)
        wsum = SB(esR3, "wsum", [128, 1], F32)
        qitok = SB(esR3, "qitok", [128, 256], BF16)
        dve_qi, ev_qiT = {}, {}
        for i in range(OT):
            tcol = i * 128
            if i >= 1:
                c.wait("tensor", "dve", dve_qi[i - 1])
            for dc in range(8):
                c.emit("tensor", lambda t, dc=dc, tcol=tcol: t.matmul(
                    psV2[:, 0:256], xnO[:, dc * OWN + tcol:dc * OWN + tcol + 128],
                    wstg2[:, dc * D:dc * D + 256], start=(dc == 0), stop=(dc == 7)))
            for dc in range(8):
                c.emit("tensor", lambda t, dc=dc, tcol=tcol: t.matmul(
                    psV2[:, 512:516], xnO[:, dc * OWN + tcol:dc * OWN + tcol + 128],
                    wstg2[:, dc * D + 256:dc * D + 260], start=(dc == 0),
                    stop=(dc == 7)), inc="pe" if dc == 7 else None)
            c.wait("vector", "pe")
            c.emit("vector", lambda v: v.reduce_max(
                out=wlog[:], in_=psV2[:, 512:516], axis=AX, negate=True), inc="dve")
            c.wait("scalar", "dve")
            c.emit("scalar", lambda s: s.activation(
                out=w01[:], in_=psV2[:, 512:516], func=AF.Exp, bias=wlog[:],
                scale=1.0), inc="act")
            c.wait("vector", "act")
            c.emit("vector", lambda v: v.reduce_sum(
                out=wsum[:], in_=w01[:], axis=AX), inc="dve")
            c.wait("vector", "dve")
            c.emit("vector", lambda v: v.reciprocal(out=wsum[:], in_=wsum[:]),
                   inc="dve")
            c.wait("vector", "dve")
            for hi in range(4):
                c.emit("vector", lambda v, hi=hi: v.tensor_scalar(
                    out=qitok[:, hi * 64:(hi + 1) * 64],
                    in0=psV2[:, hi * 64:(hi + 1) * 64],
                    scalar1=w01[:, hi:hi + 1], scalar2=wsum[:], op0=ALU.mult,
                    op1=ALU.mult), inc="dve" if hi == 3 else None)
            dve_qi[i] = c.val("dve")
            c.wait("tensor", "dve", dve_qi[i])
            if i >= 2:
                c.wait("tensor", "act", ev_qiT[i - 2])
            pq = [psA2, psB2][i % 2]
            for jj in range(2):
                c.emit("tensor", lambda t, jj=jj, pq=pq: t.transpose(
                    pq[:, jj * 128:(jj + 1) * 128], qitok[:, jj * 128:(jj + 1) * 128],
                    ident_bf[:]), inc="pe")
            c.wait("scalar", "pe")
            c.emit("scalar", lambda s, i=i, pq=pq: s.copy(
                out=qiT.rearrange("p (jj q) -> p jj q", jj=2)[:, :, i * 128:(i + 1) * 128],
                in_=pq[:, 0:256].rearrange("p (jj q) -> p jj q", jj=2)), inc="act")
            ev_qiT[i] = c.val("act")
        c.barrier()

        if upto == "B2":
            dump("qT", qT[:, 0:4096])
            dump("qiT", qiT[:, 0:4096])
            dump("biash", bias_h[:])
            dump("xnO", xnO[:, 0:2048])
            dump("o2TX", o2TX[:])
            dump("o1T", o1T[:])
        esR3.close()

    if upto in ("C", "D"):
        esC = ExitStack()
        idx_sb = SB(esC, "idx_sb", [128, T], F32)
        m8b = SB(esC, "m8b", [128, 64], F32)
        iotb = SB(esC, "iotb", [128, 256], F32)
        futu8 = SB(esC, "futu8", [128, 256], U8)
        maskneg = SB(esC, "maskneg", [128, T], BF16)
        mnT = SB(esC, "mnT", [128, 32 * 256], BF16)
        PTb = [SB(esC, f"PT{i}", [128, 1024], BF16) for i in range(2)]
        vstage = [SB(esC, f"vstage{i}", [128, 4 * 260], BF16) for i in range(2)]
        ctxev = SB(esC, "ctxev", [64, 256], BF16)
        zrow = SB(esC, "zrow", [1, 256], F32)
        zbsb = SB(esC, "zbsb", [64, 256], F32)

        for g_ in range(8):
            c.barrier()
            Wa, Wb = 4 * g_ + 2, 4 * g_ + 4
            Wg = Wb
            esI = ExitStack()
            psI = [PS(esI, f"psI{g_}_{k}", [128, 512], F32) for k in range(4)]
            psTa = PS(esI, f"psTa{g_}", [128, 1024], BF16)
            for ti, W in ((2 * g_, Wa), (2 * g_ + 1, Wb)):
                c.barrier()
                nk = W * 128
                spans = (nk + 511) // 512
                for sp in range(spans):
                    n = min(512, nk - sp * 512)
                    for hi in range(4):
                        base = 64 * (hi % 2)
                        jj = hi // 2
                        c.emit("tensor", lambda t, hi=hi, base=base, jj=jj, ti=ti,
                               sp=sp, n=n: t.matmul(
                            psI[hi][:, 0:n],
                            qiT[base:base + 64, jj * OWN + ti * 128:jj * OWN + (ti + 1) * 128],
                            kiT[base:base + 64, sp * 512:sp * 512 + n],
                            start=True, stop=True), inc="pe" if hi == 3 else None)
                    c.wait("scalar", "pe")
                    c.emit("scalar", lambda s, sp=sp, n=n: s.activation(
                        out=idx_sb[:, sp * 512:sp * 512 + n], in_=psI[0][:, 0:n],
                        func=AF.Relu), inc="act")
                    c.wait("vector", "act")
                    for hi in range(1, 4):
                        c.emit("vector", lambda v, hi=hi, sp=sp, n=n: v.scalar_tensor_tensor(
                            out=idx_sb[:, sp * 512:sp * 512 + n], in0=psI[hi][:, 0:n],
                            scalar=0.0, in1=idx_sb[:, sp * 512:sp * 512 + n],
                            op0=ALU.max, op1=ALU.add), inc="dve")
                        c.wait("vector", "dve")
                    c.wait("tensor", "dve")
                    c.wait("tensor", "act")
                c.emit("gpsimd", lambda g, W=W: g.iota(
                    iotb[:], [[1, 256]], base=(W - 2) * 128, channel_multiplier=0,
                    allow_small_or_imprecise_dtypes=True), inc="pool")
                c.wait("vector", "pool")
                c.emit("vector", lambda v, ti=ti: v.tensor_scalar(
                    out=futu8[:], in0=iotb[:], scalar1=posq_sb[:, ti:ti + 1],
                    scalar2=None, op0=ALU.is_gt), inc="dve")
                c.wait("vector", "dve")
                c.emit("vector", lambda v, nk=nk: v.copy_predicated(
                    out=idx_sb[:, nk - 256:nk], mask=futu8[:], data=neg1f[:]),
                    inc="dve")
                c.wait("vector", "dve")
                for r in range(8):
                    c.emit("vector", lambda v, r=r, nk=nk: v.max(
                        out=m8b[:, 8 * r:8 * r + 8], in_=idx_sb[:, 0:nk]), inc="tk")
                    c.wait("vector", "tk")
                    c.emit("vector", lambda v, r=r, nk=nk: v.match_replace(
                        out=idx_sb[:, 0:nk], in_to_replace=m8b[:, 8 * r:8 * r + 8],
                        in_values=idx_sb[:, 0:nk], imm_value=-2.0), inc="tk")
                    c.wait("vector", "tk")
                if g_ == 1 and ti == 2 and "idxA" in dbg:
                    dump("idxA", idx_sb[:, 0:768])
                c.emit("vector", lambda v, nk=nk: v.tensor_scalar(
                    out=maskneg[:, 0:nk], in0=idx_sb[:, 0:nk], scalar1=-2.0,
                    scalar2=NEG_BIG, op0=ALU.not_equal, op1=ALU.mult), inc="dve")
                c.wait("vector", "dve")
                c.emit("vector", lambda v, nk=nk: v.copy_predicated(
                    out=maskneg[:, nk - 256:nk], mask=futu8[:], data=negbigb[:]),
                    inc="dve")
                c.wait("tensor", "dve")
                toff = (ti % 2) * 128
                mnT3 = mnT.rearrange("p (kc q) -> p kc q", q=256)
                for kq in range((W + 7) // 8):
                    nkc = min(8, W - kq * 8)
                    if kq > 0:
                        c.wait("tensor", "act")
                    for k2 in range(nkc):
                        kc = kq * 8 + k2
                        c.emit("tensor", lambda t, kc=kc, k2=k2: t.transpose(
                            psTa[:, k2 * 128:(k2 + 1) * 128],
                            maskneg[:, kc * 128:(kc + 1) * 128], ident_bf[:]),
                            inc="pe" if k2 == nkc - 1 else None)
                    c.wait("scalar", "pe")
                    c.emit("scalar", lambda s, kq=kq, nkc=nkc, toff=toff, mnT3=mnT3:
                           s.copy(out=mnT3[:, kq * 8:kq * 8 + nkc, toff:toff + 128],
                                  in_=psTa[:, 0:nkc * 128].rearrange(
                                      "p (kc n) -> p kc n", n=128)), inc="act")
            c.wait("gpsimd", "act")
            c.emit("gpsimd", lambda g, Wa=Wa: g.memset(
                mnT.rearrange("p (kc q) -> p kc q", q=256)[:, Wa:Wa + 2, 0:128],
                NEG_BIG), inc="pool")
            esI.close()
            c.barrier()

            if g_ == 1 and "mnTg" in dbg:
                dump("mnTg", mnT[:, 0:8 * 256])
            esH = ExitStack()
            scQ = [PS(esH, f"scQ{g_}_{k}", [128, 1024], F32) for k in range(2)]
            ctxP = PS(esH, f"ctxP{g_}", [65, 1024], F32)
            zbP = PS(esH, f"zbP{g_}", [64, 256], F32)
            c.wait("tensor", "pool")
            nquad = (Wg + 3) // 4
            qq = 0
            exp_done = {}
            pv_done = {}
            vdma = {}
            m_lin = 0
            vq_done = {}
            for hgrp in range(4):
                if hgrp > 0:
                    c.wait("tensor", "dve")
                for piece in range(2):
                    c.emit("tensor", lambda t, piece=piece: t.matmul(
                        ctxP[:, piece * 512:(piece + 1) * 512],
                        zer_bf[0:1, 0:65], zer_bf[0:1, 0:512],
                        start=True, stop=True), inc="pe" if piece == 1 else None)
                for quad in range(nquad):
                    kcs = list(range(quad * 4, min(quad * 4 + 4, Wg)))
                    vb = m_lin % 2
                    if m_lin >= 2:
                        c.wait("sync", "pe", vq_done[m_lin - 2])
                    for k2, kc in enumerate(kcs):
                        c.emit("sync", lambda sy, kc=kc, k2=k2, vb=vb, hgrp=hgrp:
                               sy.dma_start(
                            out=vstage[vb][:, k2 * 260:(k2 + 1) * 260],
                            in_=v_scr[kc * 128:(kc + 1) * 128,
                                      hgrp * 260:(hgrp + 1) * 260]), inc="dql", n=16)
                    vdma[(hgrp, quad)] = c.val("dql")
                    for hh in range(4):
                        h = hgrp * 4 + hh
                        j, base = h // 2, 64 * (h % 2)
                        bank = scQ[qq % 2]
                        if qq >= 2:
                            c.wait("tensor", "act", exp_done[qq - 2])
                        for k2, kc in enumerate(kcs):
                            c.emit("tensor", lambda t, kc=kc, k2=k2, j=j, base=base,
                                   bank=bank, g_=g_: t.matmul(
                                bank[:, k2 * 256:(k2 + 1) * 256],
                                kT[base:base + 64, j * T + kc * 128:j * T + (kc + 1) * 128],
                                qT[base:base + 64, j * OWN + g_ * 256:j * OWN + (g_ + 1) * 256],
                                start=True, stop=False))
                            c.emit("tensor", lambda t, kc=kc, k2=k2, bank=bank: t.matmul(
                                bank[:, k2 * 256:(k2 + 1) * 256], ident_bf[:],
                                mnT[:, kc * 256:(kc + 1) * 256],
                                start=False, stop=True),
                                inc="pe" if k2 == len(kcs) - 1 else None)
                        mmd = c.val("pe")
                        if g_ == 0 and qq == 0 and "SCR" in dbg:
                            c.barrier()
                            c.emit("scalar", lambda s_, bank=bank: s_.copy(
                                out=idx_sb[:, 0:1024], in_=bank[:]), inc="act")
                            c.barrier()
                            c.emit("gpsimd", lambda g: g.dma_start(
                                out=dbg["SCR"], in_=idx_sb[:, 0:1024]), inc="dqo",
                                n=16)
                            c.barrier()
                        c.wait("scalar", "pe", mmd)
                        if qq >= 2:
                            c.wait("scalar", "pe", pv_done.get(qq - 2, 0))
                        c.emit("scalar", lambda s, bank=bank, h=h, qq=qq, n4=len(kcs):
                               s.activation(
                            out=PTb[qq % 2][:, 0:n4 * 256], in_=bank[:, 0:n4 * 256],
                            func=AF.Exp, bias=bias_h[:, h:h + 1], scale=0.125),
                            inc="act")
                        exp_done[qq] = c.val("act")
                        c.wait("tensor", "act", exp_done[qq])
                        c.wait("tensor", "dql", vdma[(hgrp, quad)])
                        first = (quad == 0)
                        last = (quad == nquad - 1)
                        for k2, kc in enumerate(kcs):
                            c.emit("tensor", lambda t, k2=k2, hh=hh, vb=vb, qq=qq:
                                   t.matmul(
                                ctxP[:, hh * 256:(hh + 1) * 256],
                                vstage[vb][:, k2 * 260 + hh * 65:k2 * 260 + hh * 65 + 65],
                                PTb[qq % 2][:, k2 * 256:(k2 + 1) * 256],
                                start=False, stop=True),
                                inc="pe" if k2 == len(kcs) - 1 else None)
                        pv_done[qq] = c.val("pe")
                        qq += 1
                    vq_done[m_lin] = c.val("pe")
                    m_lin += 1
                if g_ == 0 and hgrp == 0 and "Zr0" in dbg:
                    c.barrier()
                    c.emit("scalar", lambda s_: s_.copy(
                        out=zrow[:], in_=ctxP[64:65, 0:256]), inc="act")
                    c.barrier()
                    c.emit("gpsimd", lambda g: g.dma_start(
                        out=dbg["Zr0"], in_=zrow[:]), inc="dqo", n=16)
                    c.barrier()
                for hh in range(4):
                    h = hgrp * 4 + hh
                    c.wait("vector", "pe", pv_done[ (hgrp * nquad + nquad - 1 - 0) ] if False else c.val("pe"))
                    c.emit("vector", lambda v, hh=hh: v.reciprocal(
                        out=zrow[:], in_=ctxP[64:65, hh * 256:(hh + 1) * 256]),
                        inc="dve")
                    c.wait("tensor", "dve")
                    c.emit("tensor", lambda t: t.matmul(
                        zbP[:], ones1r[0:1, 0:64], zrow[:], start=True, stop=True),
                        inc="pe")
                    c.wait("scalar", "pe")
                    c.wait("scalar", "dve")
                    c.emit("scalar", lambda s: s.copy(out=zbsb[:], in_=zbP[:]),
                           inc="act")
                    c.wait("vector", "act")
                    c.emit("vector", lambda v, hh=hh: v.scalar_tensor_tensor(
                        out=ctxev[:], in0=ctxP[0:64, hh * 256:(hh + 1) * 256],
                        scalar=1.0, in1=zbsb[:], op0=ALU.mult, op1=ALU.mult),
                        inc="dve")
                    c.wait("sync", "dve")
                    c.emit("sync", lambda sy, h=h, g_=g_: sy.dma_start(
                        out=ctx_scr[h, :, g_ * 256:(g_ + 1) * 256], in_=ctxev[:]),
                        inc="dqo", n=16)
                    c.wait("vector", "dqo")
            esH.close()
        c.barrier()
        esC.close()

    if upto == "C" and "vsd" in dbg:
        c.barrier()
        c.emit("gpsimd", lambda g: g.dma_start(
            out=dbg["vsd"], in_=v_scr[0:128, 0:260]), inc="dqo", n=16)
    if upto == "C" and "ctxd" in dbg:
        c.barrier()
        c.emit("gpsimd", lambda g: g.dma_start(
            out=dbg["ctxd"], in_=ctx_scr.rearrange("h d q -> (h d) q")),
            inc="dqo", n=16)

    if "e1scr" in dbg:
        c.barrier()
        c.emit("gpsimd", lambda g: g.dma_start(out=dbg["e1scr"], in_=e1_scr),
               inc="dqo", n=16)
        c.emit("gpsimd", lambda g: g.dma_start(out=dbg["xn0d"], in_=xn0_scr[0:16, :]),
               inc="dqo", n=16)

    esR1b.close()
    esR1.close()
    if upto == "D":
        esD = ExitStack()
        w2tb = SB(esD, "w2tb", [128, 32 * D], BF16)
        wo_sb = SB(esD, "wo_sb", [64, 16 * D], BF16)
        gTg = SB(esD, "gTg", [128, 32 * 512], BF16)
        ctxg = SB(esD, "ctxg", [64, 16 * 128], BF16)
        xres0 = SB(esD, "xres0", [128, D], F32)
        xres = [xres0, xres0]
        outsb0 = SB(esD, "outsb0", [128, D], F32)
        outsb = [outsb0, outsb0]
        e1sD = SB(esD, "e1sD", [1, HI * D], BF16)
        e2sD = SB(esD, "e2sD", [16, D], BF16)
        brow = SB(esD, "brow", [1, D], F32)
        b2r = outsb[0][0:1, :]
        w1c = [SB(esD, f"w1c{i}", [128, 8 * 128], BF16) for i in range(2)]
        p1sb = SB(esD, "p1sb", [4, 512], BF16)
        p1f = SB(esD, "p1f", [1, 4 * 512], BF16)
        gstok = SB(esD, "gstok", [128, 64], BF16)
        gs16 = SB(esD, "gs16", [16, 4 * 128], BF16)

        c.emit("sync", lambda sy: sy.dma_start(out=brow[:], in_=bo), inc="dql", n=16)
        c.emit("sync", lambda sy: sy.dma_start(out=b2r[:], in_=b2), inc="dql", n=16)
        for hc in range(32):
            c.emit("gpsimd", lambda g, hc=hc: g.dma_start(
                out=w2tb[:, hc * D:(hc + 1) * D],
                in_=w2t[hc * 128:(hc + 1) * 128, :]), inc="dqg", n=16)
        for h in range(16):
            c.emit("gpsimd", lambda g, h=h: g.dma_start(
                out=wo_sb[0:64, h * D:(h + 1) * D],
                in_=wo[h * 64:(h + 1) * 64, :]), inc="dqg", n=16)
        c.emit("gpsimd", lambda g: g.dma_start(out=e1sD[:], in_=e1_scr),
               inc="dqg", n=16)
        c.emit("gpsimd", lambda g: g.dma_start(out=e2sD[:], in_=e2_scr),
               inc="dqg", n=16)
        c.wait("vector", "dql")
        c.emit("vector", lambda v: v.tensor_add(out=brow[:], in0=brow[:], in1=b2r[:]),
               inc="dve")

        psH = [PS(esD, f"psH{i}", [128, 512], F32) for i in range(2)]
        psGt = PS(esD, "psGt", [128, 64], F32)
        psp1 = PS(esD, "psp1", [4, 512], F32)
        psX = PS(esD, "psX", [128, 128], BF16)
        psO0 = PS(esD, "psO0", [128, D], F32)
        psO = [psO0, psO0]

        for tg in range(4):
            c.wait("tensor", "dqg")
            if tg > 0:
                c.wait("tensor", "act")
            for dc in range(8):
                c.emit("tensor", lambda t, dc=dc, tg=tg: t.matmul(
                    psp1[:], o1T[:, dc * 4:(dc + 1) * 4],
                    xnO[:, dc * OWN + tg * 512:dc * OWN + (tg + 1) * 512],
                    start=(dc == 0), stop=(dc == 7)), inc="pe" if dc == 7 else None)
            c.wait("scalar", "pe")
            c.emit("scalar", lambda s: s.copy(out=p1sb[:], in_=psp1[:]), inc="act")
            c.wait("gpsimd", "act")
            c.emit("gpsimd", lambda g: g.dma_start(out=p1_scr, in_=p1sb[:]),
                   inc="dqo", n=16)
            c.wait("gpsimd", "dqo")
            c.emit("gpsimd", lambda g: g.dma_start(
                out=p1f[:], in_=p1_scr.rearrange("a b -> () (a b)")), inc="dqg", n=16)
            c.wait("tensor", "dqg")
            gel = {}
            w1d = {}
            for hc in range(32):
                wb_ = hc % 2
                if hc >= 2:
                    c.wait("gpsimd", "pe", w1d.get(hc - 2, 0))
                for dc in range(8):
                    c.emit("gpsimd", lambda g, hc=hc, dc=dc, wb_=wb_: g.dma_start(
                        out=w1c[wb_][:, dc * 128:(dc + 1) * 128],
                        in_=w1t[dc * 128:(dc + 1) * 128, hc * 128:(hc + 1) * 128]),
                        inc="dqg", n=16)
                c.wait("tensor", "dqg")
                if hc >= 2:
                    c.wait("tensor", "act", gel[hc - 2])
                hp = psH[wb_]
                for dc in range(8):
                    c.emit("tensor", lambda t, dc=dc, tg=tg, wb_=wb_, hp=hp: t.matmul(
                        hp[:], w1c[wb_][:, dc * 128:(dc + 1) * 128],
                        xnO[:, dc * OWN + tg * 512:dc * OWN + (tg + 1) * 512],
                        start=(dc == 0), stop=False))
                i_ = hc // 8
                c.emit("tensor", lambda t, hc=hc, i_=i_, hp=hp: t.matmul(
                    hp[:], e1sD[0:1, i_ * D + (hc % 8) * 128:i_ * D + (hc % 8) * 128 + 128],
                    p1f[0:1, i_ * 512:(i_ + 1) * 512], start=False, stop=True),
                    inc="pe")
                w1d[hc] = c.val("pe")
                c.wait("scalar", "pe", w1d[hc])
                c.emit("scalar", lambda s, hc=hc, hp=hp: s.activation(
                    out=gTg[:, hc * 512:(hc + 1) * 512], in_=hp[:], func=AF.Gelu,
                    bias=b1c_sb[:, hc:hc + 1], scale=1.0), inc="act")
                gel[hc] = c.val("act")
                c.wait("tensor", "act", gel[hc])
                if hc == 0:
                    c.emit("tensor", lambda t: t.matmul(
                        psGt[:, 0:64], zer_bf[0:1, 0:128], zer_bf[0:1, 0:64],
                        start=True, stop=False))
                for tsub in range(4):
                    c.emit("tensor", lambda t, hc=hc, tsub=tsub: t.matmul(
                        psGt[:, tsub * 16:(tsub + 1) * 16],
                        gTg[:, hc * 512 + tsub * 128:hc * 512 + (tsub + 1) * 128],
                        o2TX[:, hc * 16:(hc + 1) * 16],
                        start=False, stop=(hc == 31)),
                        inc="pe" if (hc == 31 and tsub == 3) else None)
            if tg == 0 and "gT0" in dbg:
                dump("gT0", gTg[:, 0:512])
                dump("p1d", p1f[:])
                dump("e1d", e1sD[:])
                dump("e2d", e2sD[:])
            c.wait("scalar", "pe")
            c.emit("scalar", lambda s: s.copy(out=gstok[:], in_=psGt[:]), inc="act")
            c.wait("tensor", "act")
            for tsub in range(4):
                c.emit("tensor", lambda t, tsub=tsub: t.transpose(
                    psX[0:16, 0:128], gstok[:, tsub * 16:(tsub + 1) * 16],
                    ident_bf[:]), inc="pe")
                c.wait("scalar", "pe")
                c.emit("scalar", lambda s, tsub=tsub: s.copy(
                    out=gs16[0:16, tsub * 128:(tsub + 1) * 128], in_=psX[0:16, 0:128]),
                    inc="act")
                c.wait("tensor", "act")
            if tg == 0 and "gs16d" in dbg:
                dump("gs16d", gs16[0:16, :])
            c.wait("tensor", "act")
            st_out = {}
            for tsub in range(4):
                po = psO[tsub % 2]
                c.wait("sync", "pe")
                for h in range(16):
                    c.emit("sync", lambda sy, h=h, tg=tg, tsub=tsub: sy.dma_start(
                        out=ctxg[0:64, h * 128:(h + 1) * 128],
                        in_=ctx_scr[h, :, tg * 512 + tsub * 128:
                                    tg * 512 + (tsub + 1) * 128]), inc="dql", n=16)
                ctxld = c.val("dql")
                c.wait("tensor", "dql", ctxld)
                if tsub >= 1 or tg > 0:
                    c.wait("tensor", "dve")
                for hc in range(32):
                    for piece in range(2):
                        c.emit("tensor", lambda t, hc=hc, piece=piece, tsub=tsub,
                               po=po: t.matmul(
                            po[:, piece * 512:(piece + 1) * 512],
                            gTg[:, hc * 512 + tsub * 128:hc * 512 + (tsub + 1) * 128],
                            w2tb[:, hc * D + piece * 512:hc * D + (piece + 1) * 512],
                            start=(hc == 0), stop=False))
                for h in range(16):
                    for piece in range(2):
                        c.emit("tensor", lambda t, h=h, piece=piece,
                               po=po: t.matmul(
                            po[:, piece * 512:(piece + 1) * 512],
                            ctxg[0:64, h * 128:(h + 1) * 128],
                            wo_sb[0:64, h * D + piece * 512:h * D + (piece + 1) * 512],
                            start=False, stop=False))
                for piece in range(2):
                    c.emit("tensor", lambda t, piece=piece, tsub=tsub, po=po: t.matmul(
                        po[:, piece * 512:(piece + 1) * 512],
                        gs16[0:16, tsub * 128:(tsub + 1) * 128],
                        e2sD[0:16, piece * 512:(piece + 1) * 512],
                        start=False, stop=False))
                c.wait("tensor", "dve")
                for piece in range(2):
                    c.emit("tensor", lambda t, piece=piece, po=po,
                           last=(piece == 1): t.matmul(
                        po[:, piece * 512:(piece + 1) * 512], ones1r[:],
                        brow[0:1, piece * 512:(piece + 1) * 512],
                        start=False, stop=last), inc="pe" if piece == 1 else None)
                ffd = c.val("pe")
                rb = tsub % 2
                if tsub > 0 or tg > 0:
                    c.wait("sync", "dve")
                c.emit("sync", lambda sy, tg=tg, tsub=tsub, rb=rb: sy.dma_start(
                    out=xres[rb][:], in_=x_own[tg * 512 + tsub * 128:
                                              tg * 512 + (tsub + 1) * 128, :]),
                    inc="dql", n=16)
                c.wait("vector", "dql")
                c.wait("vector", "pe", ffd)
                c.wait("vector", "dqo")
                c.emit("vector", lambda v, rb=rb, po=po: v.scalar_tensor_tensor(
                    out=outsb[rb][:], in0=po[:], scalar=1.0, in1=xres[rb][:],
                    op0=ALU.mult, op1=ALU.add), inc="dve")
                st_out[tsub] = c.val("dve")
                c.wait("sync", "dve", st_out[tsub])
                c.emit("sync", lambda sy, tg=tg, tsub=tsub, rb=rb: sy.dma_start(
                    out=out_own[tg * 512 + tsub * 128:tg * 512 + (tsub + 1) * 128, :],
                    in_=outsb[rb][:]), inc="dqo", n=16)
        esD.close()

    c.wait("sync", "dqo")
    c.barrier()

    with nc.Block() as block:
        c.finalize(block)
    es.close()
    return nc, dbg




def kernel(**inputs):
    nc, _ = build("D", debug=())
    x4 = np.ascontiguousarray(inputs["L0_attn_x"], dtype=np.float32)
    wq = np.ascontiguousarray(inputs["L0_attn_Wq"], dtype=np.float32)
    wk = np.ascontiguousarray(inputs["L0_attn_Wk"], dtype=np.float32)
    wv = np.ascontiguousarray(inputs["L0_attn_Wv"], dtype=np.float32)
    wo = np.ascontiguousarray(inputs["L0_attn_Wo"], dtype=np.float32)
    bo = np.ascontiguousarray(inputs["L0_attn_bo"], dtype=np.float32).reshape(1, D)
    wqi = np.ascontiguousarray(inputs["L0_attn_Wqi"], dtype=np.float32)
    wki = np.ascontiguousarray(inputs["L0_attn_Wki"], dtype=np.float32)
    www = np.ascontiguousarray(inputs["L0_attn_Ww"], dtype=np.float32)
    w1t = np.ascontiguousarray(np.asarray(inputs["L0_ffn_W1"], np.float32).T)
    b1c = np.ascontiguousarray(
        np.asarray(inputs["L0_ffn_b1"], np.float32).reshape(32, 128).T)
    w2t = np.ascontiguousarray(np.asarray(inputs["L0_ffn_W2"], np.float32).T)
    b2 = np.ascontiguousarray(inputs["L0_ffn_b2"], dtype=np.float32).reshape(1, D)
    scalei = np.asarray(inputs["L0_ffn_scale"], np.float32).reshape(1, 1)

    in_maps = []
    tiles_by_p = {p: list(range(p, NT, 2)) for p in (0, 1)}
    for core in range(8):
        b, p = core // 2, core % 2
        x = x4[b]
        tiles = tiles_by_p[p]
        x_own = np.ascontiguousarray(
            np.concatenate([x[t * 128:(t + 1) * 128] for t in tiles], 0))
        pos = np.zeros((128, OT), np.float32)
        for i, t in enumerate(tiles):
            pos[:, i] = t * 128 + np.arange(128)
        in_maps.append(dict(
            x_full=np.ascontiguousarray(x), x_own=x_own, pos_qt=pos,
            wq=wq, wk=wk, wv=wv, wo=wo, bo=bo, wqi=wqi, wki=wki, www=www,
            w1t=w1t, b1c=b1c, w2t=w2t, b2=b2, scalei=scalei))

    res = run_bass_kernel_spmd(nc, in_maps, core_ids=list(range(8)))
    out = np.empty((4, T, D), np.float32)
    for core in range(8):
        b, p = core // 2, core % 2
        oo = res.results[core]["out_own"]
        for i, t in enumerate(tiles_by_p[p]):
            out[b, t * 128:(t + 1) * 128] = oo[i * 128:(i + 1) * 128]
    return out


# version 02 (final)
# speedup vs baseline: 1.0270x; 1.0270x over previous
from contextlib import ExitStack
import numpy as np
import concourse.bass as bass
import concourse.mybir as mybir
from concourse.bass_utils import run_bass_kernel_spmd

F32 = mybir.dt.float32
BF16 = mybir.dt.bfloat16
U8 = mybir.dt.uint8
AF = mybir.ActivationFunctionType
ALU = mybir.AluOpType
AX = mybir.AxisListType.X

T, D = 4096, 1024
NT, OT, OWN = 32, 16, 2048
H, HD, HI, HFF = 16, 64, 4, 4096
NEG_BIG = -100000.0
EPS = 1e-5
SEMS = ("dql", "dqg", "dqo", "pe", "act", "dve", "pool", "tk")


class Ctx:
    ENGS = ("tensor", "vector", "scalar", "gpsimd", "sync")

    def __init__(self, nc, es):
        self.nc = nc
        self.ops = {e: [] for e in self.ENGS}
        self.sems = {n: es.enter_context(nc.semaphore(n)) for n in SEMS}
        self.cnt = {n: 0 for n in SEMS}

    def emit(self, eng, fn, inc=None, n=1):
        if inc is not None:
            h = self.sems[inc]
            self.cnt[inc] += n
            self.ops[eng].append(lambda e, fn=fn, h=h, n=n: fn(e).then_inc(h, n))
            return self.cnt[inc]
        self.ops[eng].append(fn)
        return None

    def wait(self, eng, sem, val=None):
        h = self.sems[sem]
        v = self.cnt[sem] if val is None else val
        if v > 0:
            self.ops[eng].append(lambda e, h=h, v=v: e.wait_ge(h, v))

    def val(self, sem):
        return self.cnt[sem]

    def barrier(self):
        snap = dict(self.cnt)
        for eng in self.ENGS:
            for s, v in snap.items():
                if v > 0:
                    h = self.sems[s]
                    self.ops[eng].append(lambda e, h=h, v=v: e.wait_ge(h, v))

    def finalize(self, block):
        def runner(lst):
            def _(eng):
                for f in lst:
                    f(eng)
            return _
        for ename in self.ENGS:
            getattr(block, ename)(runner(self.ops[ename]))


def build(upto="D", debug=()):
    nc = bass.Bass("TRN2", target_bir_lowering=False)

    def din(name, shape, dtype=F32):
        return nc.dram_tensor(name, shape, dtype, kind="ExternalInput").ap()

    x_full = din("x_full", [T, D]); x_own = din("x_own", [OWN, D])
    pos_qt = din("pos_qt", [128, OT])
    wq = din("wq", [D, D]); wk = din("wk", [D, D]); wv = din("wv", [D, D])
    wo = din("wo", [D, D]); bo = din("bo", [1, D])
    wqi = din("wqi", [D, HI * HD]); wki = din("wki", [D, HD]); www = din("www", [D, HI])
    w1t = din("w1t", [D, HFF]); b1c = din("b1c", [128, 32])
    w2t = din("w2t", [HFF, D]); b2 = din("b2", [1, D])
    scalei = din("scalei", [1, 1])
    out_own = nc.dram_tensor("out_own", [OWN, D], F32, kind="ExternalOutput").ap()

    v_scr = nc.dram_tensor("v_scr", [T, H * 65], BF16, kind="Internal").ap()
    xn0_scr = nc.dram_tensor("xn0_scr", [128, D], BF16, kind="Internal").ap()
    ctx_scr = nc.dram_tensor("ctx_scr", [H, HD, OWN], BF16, kind="Internal").ap()
    p1_scr = nc.dram_tensor("p1_scr", [4, 512], BF16, kind="Internal").ap()
    e1_scr = nc.dram_tensor("e1_scr", [1, HI * D], BF16, kind="Internal").ap()
    e2_scr = nc.dram_tensor("e2_scr", [16, D], BF16, kind="Internal").ap()

    dbg = {}
    for name, shape, dtype in debug:
        dbg[name] = nc.dram_tensor("dbg_" + name, shape, dtype,
                                   kind="ExternalOutput").ap()

    es = ExitStack()
    c = Ctx(nc, es)

    def SB(stack, name, shape, dtype):
        return stack.enter_context(nc.sbuf_tensor(name, shape, dtype))

    def PS(stack, name, shape, dtype=F32):
        return stack.enter_context(nc.psum_tensor(name, shape, dtype))

    def dump(name, ap_src):
        if name in dbg:
            c.barrier()
            c.emit("gpsimd", lambda g, s=ap_src, d=dbg[name]: g.dma_start(
                out=d, in_=s), inc="dqo", n=16)

    ident_bf = SB(es, "ident_bf", [128, 128], BF16)
    ones1r = SB(es, "ones1r", [1, 128], F32)
    ones2c = SB(es, "ones2c", [128, 2], BF16)
    posq_sb = SB(es, "posq_sb", [128, OT], F32)
    b1c_sb = SB(es, "b1c_sb", [128, 32], F32)
    bias_h = SB(es, "bias_h", [128, H], F32)
    xnO = SB(es, "xnO", [128, 8 * OWN], BF16)
    o1T = SB(es, "o1T", [128, 8 * 4], BF16)
    o2raw = SB(es, "o2raw", [128, 32], BF16)
    o2TX = SB(es, "o2TX", [128, 32 * 16], BF16)
    scale1 = SB(es, "scale1", [1, 1], F32)
    scale4 = SB(es, "scale4", [4, 1], F32)
    neg1f = SB(es, "neg1f", [128, 256], F32)
    zer_bf = SB(es, "zer_bf", [1, 512], BF16)
    epsc = SB(es, "epsc", [128, 1], F32)
    negbigb = SB(es, "negbigb", [128, 256], BF16)

    def g0(g):
        g.memset(ident_bf[:], 0.0)
        g.memset(ones1r[:], 1.0)
        g.memset(ones2c[:], 0.0)
        g.memset(ones2c[0:64, 0:1], 1.0)
        g.memset(ones2c[64:128, 1:2], 1.0)
        g.memset(neg1f[:], -1.0)
        g.memset(zer_bf[:], 0.0)
        g.memset(epsc[:], EPS)
        return g.memset(negbigb[:], NEG_BIG)
    import os as _os
    _skip = _os.environ.get("DKSKIP", "")
    if "g0" not in _skip:
        c.emit("gpsimd", g0, inc="pool")
        c.wait("gpsimd", "pool")
    if "af" not in _skip:
        c.wait("gpsimd", "pool")
        c.emit("gpsimd", lambda g: g.affine_select(
            out=ident_bf[:], in_=ident_bf[:], compare_op=ALU.not_equal, fill=1.0,
            base=0, pattern=[[-1, 128]], channel_multiplier=1), inc="pool")

    if "s0" not in _skip:
        c.emit("sync", lambda sy: sy.dma_start(out=posq_sb[:], in_=pos_qt),
               inc="dql", n=16)
        c.emit("sync", lambda sy: sy.dma_start(out=b1c_sb[:], in_=b1c),
               inc="dql", n=16)
        c.emit("sync", lambda sy: sy.dma_start(out=scale1[:], in_=scalei),
               inc="dql", n=16)

    esR1 = ExitStack()
    kT = SB(esR1, "kT", [128, 8 * T], BF16)
    kiT = SB(esR1, "kiT", [128, T], BF16)

    esR2 = ExitStack()
    xnT = SB(esR2, "xnT", [128, 8 * T], BF16)
    wstg = SB(esR2, "wstg", [128, 8 * D], BF16)
    vstg = [SB(esR2, f"vstg{i}", [128, H * 65], BF16) for i in range(2)]
    xstg = [SB(esR2, f"xstg{i}", [128, D], F32) for i in range(2)]
    lncols = SB(esR2, "lncols", [128, 48], F32)
    sumsq, s1c, mcol, vcol, mm2, rstd = (lncols[:, i:i + 1]
                                         for i in (0, 8, 16, 24, 32, 40))
    xn_tok = [SB(esR2, f"xn_tok{i}", [128, D], BF16) for i in range(2)]
    psA = PS(esR2, "psA", [128, D], BF16)
    psB = PS(esR2, "psB", [128, D], BF16)
    psQ = PS(esR2, "psQ", [128, D], F32)
    psP = [PS(esR2, f"psP{i}", [128, 512], F32) for i in range(2)]
    psV = psQ
    psS = PS(esR2, "psS", [128, 512], F32)

    def ln_tiles(src_ap, ntiles, dst_sb, save0, B):
        psA, psB, psQ = B["psA"], B["psB"], B["psQ"]
        xstg, xn_tok = B["xstg"], B["xn_tok"]
        sumsq, s1c, mcol, vcol, mm2, rstd = B["cols"]
        psum = [psA, psB]
        dst3 = dst_sb.rearrange("p (cc n) -> p cc n", cc=8)
        ln_done, ev_done = {}, {}
        sv_dqo = [0]
        for tt in range(ntiles):
            buf = tt % 2
            if tt >= 2:
                c.wait("sync", "dve", ln_done[tt - 2])
            c.emit("sync", lambda sy, tt=tt, buf=buf: sy.dma_start(
                out=xstg[buf][:], in_=src_ap[tt * 128:(tt + 1) * 128, :]),
                inc="dql", n=16)
            ldq = c.val("dql")
            c.wait("scalar", "dql", ldq)
            if tt >= 1:
                c.wait("scalar", "dve", ln_done[tt - 1])
            c.emit("scalar", lambda s, buf=buf: s.activation(
                out=psQ[:], in_=xstg[buf][:], func=AF.Square), inc="act")
            a_sq = c.val("act")
            c.wait("vector", "dql", ldq)
            c.wait("vector", "act", a_sq)
            c.emit("vector", lambda v: v.reduce_sum(
                out=sumsq[:], in_=psQ[:], axis=AX), inc="dve")
            c.emit("vector", lambda v, buf=buf: v.reduce_sum(
                out=s1c[:], in_=xstg[buf][:], axis=AX), inc="dve")
            c.wait("vector", "dve")
            c.emit("vector", lambda v: v.tensor_scalar(
                out=mcol[:], in0=s1c[:], scalar1=1.0 / D, scalar2=None,
                op0=ALU.mult), inc="dve")
            c.emit("vector", lambda v: v.tensor_scalar(
                out=vcol[:], in0=sumsq[:], scalar1=1.0 / D, scalar2=None,
                op0=ALU.mult), inc="dve")
            c.wait("vector", "dve")
            c.emit("vector", lambda v: v.scalar_tensor_tensor(
                out=mm2[:], in0=mcol[:], scalar=-1.0, in1=mcol[:],
                op0=ALU.mult, op1=ALU.mult), inc="dve")
            c.wait("vector", "dve")
            c.emit("vector", lambda v: v.tensor_add(
                out=vcol[:], in0=vcol[:], in1=mm2[:]), inc="dve")
            c.wait("scalar", "dve")
            c.emit("scalar", lambda s: s.activation(
                out=rstd[:], in_=vcol[:], func=AF.Ln, bias=epsc[:], scale=1.0),
                inc="act")
            c.wait("scalar", "act")
            c.emit("scalar", lambda s: s.activation(
                out=rstd[:], in_=rstd[:], func=AF.Exp, bias=0.0, scale=-0.5), inc="act")
            c.wait("vector", "act")
            if tt >= 2:
                c.wait("vector", "pe", tr_done[tt - 2])
            c.emit("vector", lambda v, buf=buf: v.tensor_scalar(
                out=xn_tok[buf][:], in0=xstg[buf][:], scalar1=mcol[:],
                scalar2=rstd[:], op0=ALU.subtract, op1=ALU.mult), inc="dve")
            ln_done[tt] = c.val("dve")
            c.wait("tensor", "dve", ln_done[tt])
            if tt >= 2:
                c.wait("tensor", "act", ev_done[tt - 2])
            for j in range(8):
                c.emit("tensor", lambda t, buf=buf, j=j: t.transpose(
                    psum[buf][:, j * 128:(j + 1) * 128],
                    xn_tok[buf][:, j * 128:(j + 1) * 128], ident_bf[:]), inc="pe")
            tr_done[tt] = c.val("pe")
            c.wait("vector", "pe", tr_done[tt])
            c.wait("scalar", "pe", tr_done[tt])
            c.emit("scalar", lambda s, tt=tt, buf=buf: s.copy(
                out=dst3[:, :, tt * 128:(tt + 1) * 128],
                in_=psum[buf][:].rearrange("p (cc n) -> p cc n", cc=8)), inc="act")
            ev_done[tt] = c.val("act")
            if save0 and tt == 0:
                c.wait("gpsimd", "dve", ln_done[0])
                c.emit("gpsimd", lambda g: g.dma_start(
                    out=xn0_scr, in_=xn_tok[0][:]), inc="dqo", n=16)
                sv_dqo[0] = c.val("dqo")
            if tt == 1 and sv_dqo[0]:
                c.wait("vector", "dqo", sv_dqo[0])

    tr_done = {}
    c.wait("vector", "pool")
    c.wait("tensor", "pool")
    n_ln = NT
    if upto.startswith("A") and len(upto) > 1:
        n_ln = int(upto[1:])
    if upto.startswith("Bw") and len(upto) > 2:
        n_ln = int(upto[2:])
    lnB1 = dict(psA=psA, psB=psB, psQ=psQ, xstg=xstg, xn_tok=xn_tok,
                cols=(sumsq, s1c, mcol, vcol, mm2, rstd))
    ln_tiles(x_full, n_ln, xnT, save0=True, B=lnB1)
    c.barrier()
    if upto.startswith("A") and len(upto) > 1:
        dump("xnT", xnT[:, 0:2048])
        dump("xntok", xn_tok[0][:])
        dump("sumsq", sumsq[:])
        dump("s1c", s1c[:])
        dump("vcol", vcol[:])
        dump("rstd", rstd[:])
        dump("mcol", mcol[:])
        dump("mm2", mm2[:])
        dump("xstg0", xstg[0][:])
    if upto == "A":
        dump("xnT", xnT[:, 0:2048])
        dump("sumsq", sumsq[:])
        dump("s1c", s1c[:])
        dump("vcol", vcol[:])
        dump("rstd", rstd[:])
        dump("xntok", xn_tok[1][:])

    if upto.startswith("A") and len(upto) > 1:
        c.wait("sync", "dqo")
        c.barrier()
        esR2.close()
        esR1.close()
        with nc.Block() as block:
            c.finalize(block)
        es.close()
        return nc, dbg

    def load_w(dram_ap, ncols):
        for dc in range(8):
            c.emit("gpsimd", lambda g, dc=dc: g.dma_start(
                out=wstg[:, dc * D:dc * D + ncols],
                in_=dram_ap[dc * 128:(dc + 1) * 128, :]), inc="dqg", n=16)

    lvl = {"Bz": -1, "Bw": 0, "Bk": 1, "Bv": 2, "Bki": 3}.get(
        "Bw" if upto.startswith("Bw") else upto, 9)
    if upto != "A":
        if lvl >= 0:
            load_w(wk, D)
        c.wait("tensor", "dqg")
        ev = {}
        for j in range(8 if lvl >= 1 else 0):
            for piece in range(8):
                it = j * 8 + piece
                pp = psP[it % 2]
                if it >= 2:
                    c.wait("tensor", "act", ev[it - 2])
                for dc in range(8):
                    c.emit("tensor", lambda t, j=j, piece=piece, dc=dc, pp=pp: t.matmul(
                        pp[:], wstg[:, dc * D + j * 128:dc * D + (j + 1) * 128],
                        xnT[:, dc * T + piece * 512:dc * T + (piece + 1) * 512],
                        start=(dc == 0), stop=(dc == 7)), inc="pe" if dc == 7 else None)
                c.wait("scalar", "pe")
                c.emit("scalar", lambda s, j=j, piece=piece, pp=pp: s.copy(
                    out=kT[:, j * T + piece * 512:j * T + (piece + 1) * 512],
                    in_=pp[:]), inc="act")
                ev[it] = c.val("act")

        c.wait("gpsimd", "pe")
        if lvl >= 2:
         load_w(wv, D)
         c.wait("tensor", "dqg")
         for b_ in range(2):
             c.emit("gpsimd", lambda g, b_=b_: g.memset(
                 vstg[b_].rearrange("p (h e) -> p h e", e=65)[:, :, 64:65], 1.0),
                 inc="pool")
         ev_v, st_v = {}, {}
         for tt in range(NT):
             buf = tt % 2
             if tt >= 1:
                 c.wait("tensor", "act", ev_v[tt - 1])
             for dc in range(8):
                 for piece in range(2):
                     c.emit("tensor", lambda t, tt=tt, dc=dc, piece=piece: t.matmul(
                         psV[:, piece * 512:(piece + 1) * 512],
                         xnT[:, dc * T + tt * 128:dc * T + (tt + 1) * 128],
                         wstg[:, dc * D + piece * 512:dc * D + (piece + 1) * 512],
                         start=(dc == 0), stop=(dc == 7)),
                         inc="pe" if (dc == 7 and piece == 1) else None)
             c.wait("scalar", "pe")
             c.wait("scalar", "pool")
             if tt >= 2:
                 c.wait("scalar", "dqo", st_v[tt - 2])
             c.emit("scalar", lambda s, buf=buf: s.copy(
                 out=vstg[buf].rearrange("p (h e) -> p h e", e=65)[:, :, 0:64],
                 in_=psV[:].rearrange("p (h e) -> p h e", e=64)), inc="act")
             ev_v[tt] = c.val("act")
             c.wait("sync", "act", ev_v[tt])
             c.emit("sync", lambda sy, tt=tt, buf=buf: sy.dma_start(
                 out=v_scr[tt * 128:(tt + 1) * 128, :], in_=vstg[buf][:]),
                 inc="dqo", n=16)
             st_v[tt] = c.val("dqo")

        if lvl >= 3:
         c.wait("gpsimd", "pe")
         for dc in range(8):
             for hf in range(2):
                 c.emit("gpsimd", lambda g, dc=dc, hf=hf: g.dma_start(
                     out=wstg[:, dc * D + hf * 64:dc * D + hf * 64 + 64],
                     in_=wki[dc * 128:(dc + 1) * 128, :]), inc="dqg", n=16)
         c.wait("tensor", "dqg")
         ev = {}
         for piece in range(8):
             pp = psP[piece % 2]
             if piece >= 2:
                 c.wait("tensor", "act", ev[piece - 2])
             for dc in range(8):
                 c.emit("tensor", lambda t, piece=piece, dc=dc, pp=pp: t.matmul(
                     pp[:], wstg[:, dc * D:dc * D + 128],
                     xnT[:, dc * T + piece * 512:dc * T + (piece + 1) * 512],
                     start=(dc == 0), stop=(dc == 7)), inc="pe" if dc == 7 else None)
             c.wait("scalar", "pe")
             c.emit("scalar", lambda s, piece=piece, pp=pp: s.copy(
                 out=kiT[:, piece * 512:(piece + 1) * 512], in_=pp[:]), inc="act")
             ev[piece] = c.val("act")

        if lvl >= 4:
         for dc in range(8):
             c.emit("vector", lambda v, dc=dc: v.tensor_copy(
                 out=o1T[:, dc * 4:(dc + 1) * 4], in_=xnT[:, dc * T + 2:dc * T + 10:2]))
             c.emit("vector", lambda v, dc=dc: v.tensor_copy(
                 out=o2raw[:, dc * 4:(dc + 1) * 4], in_=xnT[:, dc * T + 10:dc * T + 18:2]),
                 inc="dve")
         c.emit("gpsimd", lambda g: g.memset(o2TX[:], 0.0), inc="pool")
         c.wait("vector", "pool")
         c.wait("vector", "dve")
         for hc in range(32):
             hi, dc = hc // 8, hc % 8
             for i in range(4):
                 c.emit("vector", lambda v, hc=hc, hi=hi, dc=dc, i=i: v.tensor_copy(
                     out=o2TX[:, hc * 16 + i * 4 + hi:hc * 16 + i * 4 + hi + 1],
                     in_=o2raw[:, dc * 4 + i:dc * 4 + i + 1]),
                     inc="dve" if (hc == 31 and i == 3) else None)

         e1tmp = vstg[0][0:1, 0:D]
         e2t = vstg[1][:, 0:8]
         e2w = vstg[1][0:4, 8:264]
         c.wait("gpsimd", "dqo")
         c.wait("tensor", "dql")
         c.emit("tensor", lambda t: t.matmul(
             psS[0:4, 16:17], ones1r[0:1, 0:4], scale1[:], start=True, stop=True),
             inc="pe")
         c.wait("scalar", "pe")
         c.emit("scalar", lambda s: s.copy(out=scale4[:], in_=psS[0:4, 16:17]),
                inc="act")
         for i in range(4):
             c.emit("gpsimd", lambda g, i=i: g.dma_start(
                 out=e1tmp[:], in_=xn0_scr[2 * i + 1:2 * i + 2, :]), inc="dqg", n=16)
             c.wait("vector", "dqg")
             c.wait("vector", "act")
             c.emit("vector", lambda v: v.tensor_scalar(
                 out=e1tmp[:], in0=e1tmp[:], scalar1=scale1[:], scalar2=None,
                 op0=ALU.mult), inc="dve")
             c.wait("gpsimd", "dve")
             c.emit("gpsimd", lambda g, i=i: g.dma_start(
                 out=e1_scr[0:1, i * D:(i + 1) * D], in_=e1tmp[:]), inc="dqo", n=16)
             c.wait("gpsimd", "dqo")
         for i in range(4):
             ro = 9 + 2 * i
             src = xn0_scr[ro:ro + 1, :].rearrange("a (r hi) -> (a r) hi", hi=4)
             for hb in range(2):
                 c.emit("gpsimd", lambda g, hb=hb, src=src: g.dma_start(
                     out=e2t[:, hb * 4:(hb + 1) * 4],
                     in_=src[hb * 128:(hb + 1) * 128, :]), inc="dqg", n=16)
             c.wait("tensor", "dqg")
             if i > 0:
                 c.wait("tensor", "dqo", c.val("dqo"))
             for hb in range(2):
                 c.emit("tensor", lambda t, hb=hb: t.transpose(
                     psA[0:4, hb * 128:(hb + 1) * 128], e2t[:, hb * 4:(hb + 1) * 4],
                     ident_bf[:]), inc="pe")
             c.wait("scalar", "pe")
             c.wait("scalar", "act")
             c.emit("scalar", lambda s: s.activation(
                 out=e2w[:], in_=psA[0:4, 0:256], func=AF.Copy, bias=0.0,
                 scale=scale4[:]), inc="act")
             c.wait("gpsimd", "act")
             c.emit("gpsimd", lambda g, i=i: g.dma_start(
                 out=e2_scr[4 * i:4 * i + 4, i * 256:(i + 1) * 256], in_=e2w[:]),
                 inc="dqo", n=16)
         zrow4 = e2w
         c.wait("vector", "dqo")
         c.emit("vector", lambda v: v.memset(e2w[:], 0.0), inc="dve")
         c.wait("gpsimd", "dve")
         for i in range(4):
             for oi in range(4):
                 if oi != i:
                     c.emit("gpsimd", lambda g, i=i, oi=oi: g.dma_start(
                         out=e2_scr[4 * i:4 * i + 4, oi * 256:(oi + 1) * 256],
                         in_=e2w[:]), inc="dqo", n=16)
         c.barrier()

    if upto in ("A", "B1", "Bz", "Bk", "Bv", "Bki") or upto.startswith("Bw"):
        dump("wstgd", wstg[:, 0:256])
        dump("wstgb", wstg[:, 0:256])
        if "wkraw" in dbg:
            c.barrier()
            c.emit("sync", lambda sy: sy.dma_start(
                out=xstg[0][:], in_=wk[0:128, :]), inc="dql", n=16)
            c.wait("gpsimd", "dql")
            c.emit("gpsimd", lambda g: g.dma_start(
                out=dbg["wkraw"], in_=xstg[0][:, 0:256]), inc="dqo", n=16)
        if "wkg" in dbg:
            c.barrier()
            c.emit("gpsimd", lambda g: g.dma_start(
                out=xstg[1][:], in_=wk[0:128, :]), inc="dqg", n=16)
            c.wait("gpsimd", "dqg")
            c.emit("gpsimd", lambda g: g.dma_start(
                out=dbg["wkg"], in_=xstg[1][:, 0:256]), inc="dqo", n=16)
        dump("xnTb", xnT[:, 0:2048])
        dump("kT", kT[:, 0:4096])
        dump("kiT", kiT[:, 0:4096])

    esR2.close()

    esR1b = ExitStack()
    qT = SB(esR1b, "qT", [128, 8 * OWN], BF16)
    qiT = SB(esR1b, "qiT", [128, 2 * OWN], BF16)

    if upto not in ("A",):
        esR3 = ExitStack()
        wstg2 = SB(esR3, "wstg3", [128, 8 * D], BF16)
        xstg2 = [SB(esR3, f"xstg3{i}", [128, D], F32) for i in range(2)]
        lncols3 = SB(esR3, "lncols3", [128, 48], F32)
        sumsq2, s1c2, mcol2, vcol2, mm22, rstd2 = (lncols3[:, i:i + 1]
                                             for i in (0, 8, 16, 24, 32, 40))
        xn_tok2 = [SB(esR3, f"xn_tok3{i}", [128, D], BF16) for i in range(2)]
        psA2 = PS(esR3, "psA2", [128, D], BF16)
        psB2 = PS(esR3, "psB2", [128, D], BF16)
        psQ2 = PS(esR3, "psQ2", [128, D], F32)
        psP2 = [PS(esR3, f"psP2{i}", [128, 512], F32) for i in range(2)]
        psV2 = psQ2
        psS2 = PS(esR3, "psS2", [128, 512], F32)

        tr_done.clear()
        lnB2 = dict(psA=psA2, psB=psB2, psQ=psQ2, xstg=xstg2, xn_tok=xn_tok2,
                    cols=(sumsq2, s1c2, mcol2, vcol2, mm22, rstd2))
        ln_tiles(x_own, OT, xnO, save0=False, B=lnB2)
        c.barrier()

        def load_w3(dram_ap, ncols):
            for dc in range(8):
                c.emit("gpsimd", lambda g, dc=dc: g.dma_start(
                    out=wstg2[:, dc * D:dc * D + ncols],
                    in_=dram_ap[dc * 128:(dc + 1) * 128, :]), inc="dqg", n=16)

        load_w3(wq, D)
        c.wait("tensor", "dqg")
        ev = {}
        for j in range(8):
            for piece in range(4):
                it = j * 4 + piece
                pp = psP2[it % 2]
                if it >= 2:
                    c.wait("tensor", "act", ev[it - 2])
                for dc in range(8):
                    c.emit("tensor", lambda t, j=j, piece=piece, dc=dc, pp=pp: t.matmul(
                        pp[:], wstg2[:, dc * D + j * 128:dc * D + (j + 1) * 128],
                        xnO[:, dc * OWN + piece * 512:dc * OWN + (piece + 1) * 512],
                        start=(dc == 0), stop=(dc == 7)), inc="pe" if dc == 7 else None)
                c.wait("scalar", "pe")
                c.emit("scalar", lambda s, j=j, piece=piece, pp=pp: s.copy(
                    out=qT[:, j * OWN + piece * 512:j * OWN + (piece + 1) * 512],
                    in_=pp[:]), inc="act")
                ev[it] = c.val("act")

        ksq = SB(esR3, "ksq", [128, 512], BF16)
        zpad = SB(esR3, "zpad", [128, 8], F32)
        kqmax = SB(esR3, "kqmax", [2, 32], F32)
        crow = SB(esR3, "crow", [1, 32], F32)
        c.emit("gpsimd", lambda g: g.memset(kqmax[:], 0.0), inc="pool")
        c.wait("vector", "pool")

        def sqmax(src_sb, src_T, npiece, outcol):
            pe_sq, dve_mx = {}, {}
            for j in range(8):
                for piece in range(npiece):
                    it = j * npiece + piece
                    if it >= 1:
                        c.wait("vector", "pe", pe_sq[it - 1])
                    c.emit("vector", lambda v, j=j, piece=piece: v.tensor_mul(
                        out=ksq[:],
                        in0=src_sb[:, j * src_T + piece * 512:j * src_T + (piece + 1) * 512],
                        in1=src_sb[:, j * src_T + piece * 512:j * src_T + (piece + 1) * 512]),
                        inc="dve")
                    c.wait("tensor", "dve")
                    if it >= 1:
                        c.wait("tensor", "dve", dve_mx[it - 1])
                    c.emit("tensor", lambda t: t.matmul(
                        psS2[0:2, 0:512], ones2c[:], ksq[:], start=True, stop=True),
                        inc="pe")
                    pe_sq[it] = c.val("pe")
                    c.wait("vector", "pe", pe_sq[it])
                    c.emit("vector", lambda v: v.reduce_max(
                        out=zpad[0:2, 0:1], in_=psS2[0:2, 0:512], axis=AX), inc="dve")
                    c.wait("vector", "dve")
                    c.emit("vector", lambda v, j=j: v.tensor_max(
                        out=kqmax[0:2, outcol + j:outcol + j + 1],
                        in0=kqmax[0:2, outcol + j:outcol + j + 1],
                        in1=zpad[0:2, 0:1]), inc="dve")
                    dve_mx[it] = c.val("dve")

        sqmax(kT, T, 8, 0)
        sqmax(qT, OWN, 4, 16)
        c.wait("vector", "dve")
        c.emit("vector", lambda v: v.tensor_add(
            out=kqmax[0:2, 0:16], in0=kqmax[0:2, 0:16], in1=kqmax[0:2, 16:32]),
            inc="dve")
        c.wait("vector", "dve")
        c.emit("vector", lambda v: v.tensor_scalar(
            out=kqmax[0:2, 0:16], in0=kqmax[0:2, 0:16], scalar1=1.0 / 16.0,
            scalar2=None, op0=ALU.mult), inc="dve")
        c.wait("gpsimd", "dve")
        c.emit("gpsimd", lambda g: g.dma_start(out=crow[:], in_=kqmax[0:2, 0:16]),
               inc="dqg", n=16)
        c.wait("tensor", "dqg")
        c.wait("tensor", "dve")
        for h in range(16):
            col = (h % 2) * 16 + h // 2
            c.emit("tensor", lambda t, h=h, col=col: t.matmul(
                psS2[:, h:h + 1], ones1r[:], crow[0:1, col:col + 1],
                start=True, stop=True), inc="pe" if h == 15 else None)
        c.wait("scalar", "pe")
        c.emit("scalar", lambda s: s.activation(
            out=bias_h[:], in_=psS2[:, 0:16], func=AF.Copy, bias=0.0, scale=-1.0),
            inc="act")

        c.wait("gpsimd", "pe")
        for dc in range(8):
            c.emit("gpsimd", lambda g, dc=dc: g.dma_start(
                out=wstg2[:, dc * D:dc * D + 256], in_=wqi[dc * 128:(dc + 1) * 128, :]),
                inc="dqg", n=16)
            c.emit("gpsimd", lambda g, dc=dc: g.dma_start(
                out=wstg2[:, dc * D + 256:dc * D + 260],
                in_=www[dc * 128:(dc + 1) * 128, :]), inc="dqg", n=16)
        c.wait("tensor", "dqg")
        wlog = SB(esR3, "wlog", [128, 1], F32)
        w01 = SB(esR3, "w01", [128, 4], F32)
        wsum = SB(esR3, "wsum", [128, 1], F32)
        qitok = SB(esR3, "qitok", [128, 256], BF16)
        dve_qi, ev_qiT = {}, {}
        for i in range(OT):
            tcol = i * 128
            if i >= 1:
                c.wait("tensor", "dve", dve_qi[i - 1])
            for dc in range(8):
                c.emit("tensor", lambda t, dc=dc, tcol=tcol: t.matmul(
                    psV2[:, 0:256], xnO[:, dc * OWN + tcol:dc * OWN + tcol + 128],
                    wstg2[:, dc * D:dc * D + 256], start=(dc == 0), stop=(dc == 7)))
            for dc in range(8):
                c.emit("tensor", lambda t, dc=dc, tcol=tcol: t.matmul(
                    psV2[:, 512:516], xnO[:, dc * OWN + tcol:dc * OWN + tcol + 128],
                    wstg2[:, dc * D + 256:dc * D + 260], start=(dc == 0),
                    stop=(dc == 7)), inc="pe" if dc == 7 else None)
            c.wait("vector", "pe")
            c.emit("vector", lambda v: v.reduce_max(
                out=wlog[:], in_=psV2[:, 512:516], axis=AX, negate=True), inc="dve")
            c.wait("scalar", "dve")
            c.emit("scalar", lambda s: s.activation(
                out=w01[:], in_=psV2[:, 512:516], func=AF.Exp, bias=wlog[:],
                scale=1.0), inc="act")
            c.wait("vector", "act")
            c.emit("vector", lambda v: v.reduce_sum(
                out=wsum[:], in_=w01[:], axis=AX), inc="dve")
            c.wait("vector", "dve")
            c.emit("vector", lambda v: v.reciprocal(out=wsum[:], in_=wsum[:]),
                   inc="dve")
            c.wait("vector", "dve")
            for hi in range(4):
                c.emit("vector", lambda v, hi=hi: v.tensor_scalar(
                    out=qitok[:, hi * 64:(hi + 1) * 64],
                    in0=psV2[:, hi * 64:(hi + 1) * 64],
                    scalar1=w01[:, hi:hi + 1], scalar2=wsum[:], op0=ALU.mult,
                    op1=ALU.mult), inc="dve" if hi == 3 else None)
            dve_qi[i] = c.val("dve")
            c.wait("tensor", "dve", dve_qi[i])
            if i >= 2:
                c.wait("tensor", "act", ev_qiT[i - 2])
            pq = [psA2, psB2][i % 2]
            for jj in range(2):
                c.emit("tensor", lambda t, jj=jj, pq=pq: t.transpose(
                    pq[:, jj * 128:(jj + 1) * 128], qitok[:, jj * 128:(jj + 1) * 128],
                    ident_bf[:]), inc="pe")
            c.wait("scalar", "pe")
            c.emit("scalar", lambda s, i=i, pq=pq: s.copy(
                out=qiT.rearrange("p (jj q) -> p jj q", jj=2)[:, :, i * 128:(i + 1) * 128],
                in_=pq[:, 0:256].rearrange("p (jj q) -> p jj q", jj=2)), inc="act")
            ev_qiT[i] = c.val("act")
        c.barrier()

        if upto == "B2":
            dump("qT", qT[:, 0:4096])
            dump("qiT", qiT[:, 0:4096])
            dump("biash", bias_h[:])
            dump("xnO", xnO[:, 0:2048])
            dump("o2TX", o2TX[:])
            dump("o1T", o1T[:])
        esR3.close()

    if upto in ("C", "D"):
        esC = ExitStack()
        idx_sb = SB(esC, "idx_sb", [128, T], F32)
        m8b = SB(esC, "m8b", [128, 64], F32)
        iotb = SB(esC, "iotb", [128, 256], F32)
        futu8 = SB(esC, "futu8", [128, 256], U8)
        maskneg = SB(esC, "maskneg", [128, T], BF16)
        mnT = SB(esC, "mnT", [128, 32 * 256], BF16)
        PTb = [SB(esC, f"PT{i}", [128, 1024], BF16) for i in range(2)]
        vstage = [SB(esC, f"vstage{i}", [128, 4 * 260], BF16) for i in range(2)]
        ctxev = SB(esC, "ctxev", [64, 256], BF16)
        zrow = SB(esC, "zrow", [1, 256], F32)
        zbsb = SB(esC, "zbsb", [64, 256], F32)

        for g_ in range(8):
            c.barrier()
            Wa, Wb = 4 * g_ + 2, 4 * g_ + 4
            Wg = Wb
            esI = ExitStack()
            psI = [PS(esI, f"psI{g_}_{k}", [128, 512], F32) for k in range(4)]
            psTa = PS(esI, f"psTa{g_}", [128, 1024], BF16)
            for ti, W in ((2 * g_, Wa), (2 * g_ + 1, Wb)):
                c.barrier()
                nk = W * 128
                spans = (nk + 511) // 512
                for sp in range(spans):
                    n = min(512, nk - sp * 512)
                    for hi in range(4):
                        base = 64 * (hi % 2)
                        jj = hi // 2
                        c.emit("tensor", lambda t, hi=hi, base=base, jj=jj, ti=ti,
                               sp=sp, n=n: t.matmul(
                            psI[hi][:, 0:n],
                            qiT[base:base + 64, jj * OWN + ti * 128:jj * OWN + (ti + 1) * 128],
                            kiT[base:base + 64, sp * 512:sp * 512 + n],
                            start=True, stop=True), inc="pe" if hi == 3 else None)
                    c.wait("scalar", "pe")
                    c.emit("scalar", lambda s, sp=sp, n=n: s.activation(
                        out=idx_sb[:, sp * 512:sp * 512 + n], in_=psI[0][:, 0:n],
                        func=AF.Relu), inc="act")
                    c.wait("vector", "act")
                    for hi in range(1, 4):
                        c.emit("vector", lambda v, hi=hi, sp=sp, n=n: v.scalar_tensor_tensor(
                            out=idx_sb[:, sp * 512:sp * 512 + n], in0=psI[hi][:, 0:n],
                            scalar=0.0, in1=idx_sb[:, sp * 512:sp * 512 + n],
                            op0=ALU.max, op1=ALU.add), inc="dve")
                        c.wait("vector", "dve")
                    c.wait("tensor", "dve")
                    c.wait("tensor", "act")
                c.emit("gpsimd", lambda g, W=W: g.iota(
                    iotb[:], [[1, 256]], base=(W - 2) * 128, channel_multiplier=0,
                    allow_small_or_imprecise_dtypes=True), inc="pool")
                c.wait("vector", "pool")
                c.emit("vector", lambda v, ti=ti: v.tensor_scalar(
                    out=futu8[:], in0=iotb[:], scalar1=posq_sb[:, ti:ti + 1],
                    scalar2=None, op0=ALU.is_gt), inc="dve")
                c.wait("vector", "dve")
                c.emit("vector", lambda v, nk=nk: v.copy_predicated(
                    out=idx_sb[:, nk - 256:nk], mask=futu8[:], data=neg1f[:]),
                    inc="dve")
                c.wait("vector", "dve")
                for r in range(8):
                    c.emit("vector", lambda v, r=r, nk=nk: v.max(
                        out=m8b[:, 8 * r:8 * r + 8], in_=idx_sb[:, 0:nk]), inc="tk")
                    c.wait("vector", "tk")
                    c.emit("vector", lambda v, r=r, nk=nk: v.match_replace(
                        out=idx_sb[:, 0:nk], in_to_replace=m8b[:, 8 * r:8 * r + 8],
                        in_values=idx_sb[:, 0:nk], imm_value=-2.0), inc="tk")
                    c.wait("vector", "tk")
                if g_ == 1 and ti == 2 and "idxA" in dbg:
                    dump("idxA", idx_sb[:, 0:768])
                c.emit("vector", lambda v, nk=nk: v.tensor_scalar(
                    out=maskneg[:, 0:nk], in0=idx_sb[:, 0:nk], scalar1=-2.0,
                    scalar2=NEG_BIG, op0=ALU.not_equal, op1=ALU.mult), inc="dve")
                c.wait("vector", "dve")
                c.emit("vector", lambda v, nk=nk: v.copy_predicated(
                    out=maskneg[:, nk - 256:nk], mask=futu8[:], data=negbigb[:]),
                    inc="dve")
                c.wait("tensor", "dve")
                toff = (ti % 2) * 128
                mnT3 = mnT.rearrange("p (kc q) -> p kc q", q=256)
                for kq in range((W + 7) // 8):
                    nkc = min(8, W - kq * 8)
                    if kq > 0:
                        c.wait("tensor", "act")
                    for k2 in range(nkc):
                        kc = kq * 8 + k2
                        c.emit("tensor", lambda t, kc=kc, k2=k2: t.transpose(
                            psTa[:, k2 * 128:(k2 + 1) * 128],
                            maskneg[:, kc * 128:(kc + 1) * 128], ident_bf[:]),
                            inc="pe" if k2 == nkc - 1 else None)
                    c.wait("scalar", "pe")
                    c.emit("scalar", lambda s, kq=kq, nkc=nkc, toff=toff, mnT3=mnT3:
                           s.copy(out=mnT3[:, kq * 8:kq * 8 + nkc, toff:toff + 128],
                                  in_=psTa[:, 0:nkc * 128].rearrange(
                                      "p (kc n) -> p kc n", n=128)), inc="act")
            c.wait("gpsimd", "act")
            c.emit("gpsimd", lambda g, Wa=Wa: g.memset(
                mnT.rearrange("p (kc q) -> p kc q", q=256)[:, Wa:Wa + 2, 0:128],
                NEG_BIG), inc="pool")
            esI.close()
            c.barrier()

            if g_ == 1 and "mnTg" in dbg:
                dump("mnTg", mnT[:, 0:8 * 256])
            esH = ExitStack()
            scQ = [PS(esH, f"scQ{g_}_{k}", [128, 1024], F32) for k in range(2)]
            ctxP = PS(esH, f"ctxP{g_}", [65, 1024], F32)
            zbP = PS(esH, f"zbP{g_}", [64, 256], F32)
            c.wait("tensor", "pool")
            nquad = (Wg + 3) // 4
            qq = 0
            exp_done = {}
            pv_done = {}
            vdma = {}
            m_lin = 0
            vq_done = {}
            for hgrp in range(4):
                if hgrp > 0:
                    c.wait("tensor", "dve")
                for piece in range(2):
                    c.emit("tensor", lambda t, piece=piece: t.matmul(
                        ctxP[:, piece * 512:(piece + 1) * 512],
                        zer_bf[0:1, 0:65], zer_bf[0:1, 0:512],
                        start=True, stop=True), inc="pe" if piece == 1 else None)
                for quad in range(nquad):
                    kcs = list(range(quad * 4, min(quad * 4 + 4, Wg)))
                    vb = m_lin % 2
                    if m_lin >= 2:
                        c.wait("sync", "pe", vq_done[m_lin - 2])
                    for k2, kc in enumerate(kcs):
                        c.emit("sync", lambda sy, kc=kc, k2=k2, vb=vb, hgrp=hgrp:
                               sy.dma_start(
                            out=vstage[vb][:, k2 * 260:(k2 + 1) * 260],
                            in_=v_scr[kc * 128:(kc + 1) * 128,
                                      hgrp * 260:(hgrp + 1) * 260]), inc="dql", n=16)
                    vdma[(hgrp, quad)] = c.val("dql")
                    for hh in range(4):
                        h = hgrp * 4 + hh
                        j, base = h // 2, 64 * (h % 2)
                        bank = scQ[qq % 2]
                        if qq >= 2:
                            c.wait("tensor", "act", exp_done[qq - 2])
                        for k2, kc in enumerate(kcs):
                            c.emit("tensor", lambda t, kc=kc, k2=k2, j=j, base=base,
                                   bank=bank, g_=g_: t.matmul(
                                bank[:, k2 * 256:(k2 + 1) * 256],
                                kT[base:base + 64, j * T + kc * 128:j * T + (kc + 1) * 128],
                                qT[base:base + 64, j * OWN + g_ * 256:j * OWN + (g_ + 1) * 256],
                                start=True, stop=False))
                            c.emit("tensor", lambda t, kc=kc, k2=k2, bank=bank: t.matmul(
                                bank[:, k2 * 256:(k2 + 1) * 256], ident_bf[:],
                                mnT[:, kc * 256:(kc + 1) * 256],
                                start=False, stop=True),
                                inc="pe" if k2 == len(kcs) - 1 else None)
                        mmd = c.val("pe")
                        if g_ == 0 and qq == 0 and "SCR" in dbg:
                            c.barrier()
                            c.emit("scalar", lambda s_, bank=bank: s_.copy(
                                out=idx_sb[:, 0:1024], in_=bank[:]), inc="act")
                            c.barrier()
                            c.emit("gpsimd", lambda g: g.dma_start(
                                out=dbg["SCR"], in_=idx_sb[:, 0:1024]), inc="dqo",
                                n=16)
                            c.barrier()
                        c.wait("scalar", "pe", mmd)
                        if qq >= 2:
                            c.wait("scalar", "pe", pv_done.get(qq - 2, 0))
                        c.emit("scalar", lambda s, bank=bank, h=h, qq=qq, n4=len(kcs):
                               s.activation(
                            out=PTb[qq % 2][:, 0:n4 * 256], in_=bank[:, 0:n4 * 256],
                            func=AF.Exp, bias=bias_h[:, h:h + 1], scale=0.125),
                            inc="act")
                        exp_done[qq] = c.val("act")
                        c.wait("tensor", "act", exp_done[qq])
                        c.wait("tensor", "dql", vdma[(hgrp, quad)])
                        first = (quad == 0)
                        last = (quad == nquad - 1)
                        for k2, kc in enumerate(kcs):
                            c.emit("tensor", lambda t, k2=k2, hh=hh, vb=vb, qq=qq:
                                   t.matmul(
                                ctxP[:, hh * 256:(hh + 1) * 256],
                                vstage[vb][:, k2 * 260 + hh * 65:k2 * 260 + hh * 65 + 65],
                                PTb[qq % 2][:, k2 * 256:(k2 + 1) * 256],
                                start=False, stop=True),
                                inc="pe" if k2 == len(kcs) - 1 else None)
                        pv_done[qq] = c.val("pe")
                        qq += 1
                    vq_done[m_lin] = c.val("pe")
                    m_lin += 1
                if g_ == 0 and hgrp == 0 and "Zr0" in dbg:
                    c.barrier()
                    c.emit("scalar", lambda s_: s_.copy(
                        out=zrow[:], in_=ctxP[64:65, 0:256]), inc="act")
                    c.barrier()
                    c.emit("gpsimd", lambda g: g.dma_start(
                        out=dbg["Zr0"], in_=zrow[:]), inc="dqo", n=16)
                    c.barrier()
                for hh in range(4):
                    h = hgrp * 4 + hh
                    c.wait("vector", "pe", pv_done[ (hgrp * nquad + nquad - 1 - 0) ] if False else c.val("pe"))
                    c.emit("vector", lambda v, hh=hh: v.reciprocal(
                        out=zrow[:], in_=ctxP[64:65, hh * 256:(hh + 1) * 256]),
                        inc="dve")
                    c.wait("tensor", "dve")
                    c.emit("tensor", lambda t: t.matmul(
                        zbP[:], ones1r[0:1, 0:64], zrow[:], start=True, stop=True),
                        inc="pe")
                    c.wait("scalar", "pe")
                    c.wait("scalar", "dve")
                    c.emit("scalar", lambda s: s.copy(out=zbsb[:], in_=zbP[:]),
                           inc="act")
                    c.wait("vector", "act")
                    c.emit("vector", lambda v, hh=hh: v.scalar_tensor_tensor(
                        out=ctxev[:], in0=ctxP[0:64, hh * 256:(hh + 1) * 256],
                        scalar=1.0, in1=zbsb[:], op0=ALU.mult, op1=ALU.mult),
                        inc="dve")
                    c.wait("sync", "dve")
                    c.emit("sync", lambda sy, h=h, g_=g_: sy.dma_start(
                        out=ctx_scr[h, :, g_ * 256:(g_ + 1) * 256], in_=ctxev[:]),
                        inc="dqo", n=16)
                    c.wait("vector", "dqo")
            esH.close()
        c.barrier()
        esC.close()

    if upto == "C" and "vsd" in dbg:
        c.barrier()
        c.emit("gpsimd", lambda g: g.dma_start(
            out=dbg["vsd"], in_=v_scr[0:128, 0:260]), inc="dqo", n=16)
    if upto == "C" and "ctxd" in dbg:
        c.barrier()
        c.emit("gpsimd", lambda g: g.dma_start(
            out=dbg["ctxd"], in_=ctx_scr.rearrange("h d q -> (h d) q")),
            inc="dqo", n=16)

    if "e1scr" in dbg:
        c.barrier()
        c.emit("gpsimd", lambda g: g.dma_start(out=dbg["e1scr"], in_=e1_scr),
               inc="dqo", n=16)
        c.emit("gpsimd", lambda g: g.dma_start(out=dbg["xn0d"], in_=xn0_scr[0:16, :]),
               inc="dqo", n=16)

    esR1b.close()
    esR1.close()
    if upto == "D":
        esD = ExitStack()
        w2tb = SB(esD, "w2tb", [128, 32 * D], BF16)
        wo_sb = SB(esD, "wo_sb", [64, 16 * D], BF16)
        gTg = SB(esD, "gTg", [128, 32 * 512], BF16)
        ctxg = SB(esD, "ctxg", [64, 16 * 128], BF16)
        xres0 = SB(esD, "xres0", [128, D], F32)
        xres = [xres0, xres0]
        outsb0 = SB(esD, "outsb0", [128, D], F32)
        outsb = [outsb0, outsb0]
        e1sD = SB(esD, "e1sD", [1, HI * D], BF16)
        e2sD = SB(esD, "e2sD", [16, D], BF16)
        brow = SB(esD, "brow", [1, D], F32)
        b2r = outsb[0][0:1, :]
        w1c = [SB(esD, f"w1c{i}", [128, 8 * 128], BF16) for i in range(2)]
        p1sb = SB(esD, "p1sb", [4, 512], BF16)
        p1f = SB(esD, "p1f", [1, 4 * 512], BF16)
        gstok = SB(esD, "gstok", [128, 64], BF16)
        gs16 = SB(esD, "gs16", [16, 4 * 128], BF16)

        c.emit("sync", lambda sy: sy.dma_start(out=brow[:], in_=bo), inc="dql", n=16)
        c.emit("sync", lambda sy: sy.dma_start(out=b2r[:], in_=b2), inc="dql", n=16)
        for hc in range(32):
            c.emit("gpsimd", lambda g, hc=hc: g.dma_start(
                out=w2tb[:, hc * D:(hc + 1) * D],
                in_=w2t[hc * 128:(hc + 1) * 128, :]), inc="dqg", n=16)
        for h in range(16):
            c.emit("gpsimd", lambda g, h=h: g.dma_start(
                out=wo_sb[0:64, h * D:(h + 1) * D],
                in_=wo[h * 64:(h + 1) * 64, :]), inc="dqg", n=16)
        c.emit("gpsimd", lambda g: g.dma_start(out=e1sD[:], in_=e1_scr),
               inc="dqg", n=16)
        c.emit("gpsimd", lambda g: g.dma_start(out=e2sD[:], in_=e2_scr),
               inc="dqg", n=16)
        c.wait("vector", "dql")
        c.emit("vector", lambda v: v.tensor_add(out=brow[:], in0=brow[:], in1=b2r[:]),
               inc="dve")

        psH = [PS(esD, f"psH{i}", [128, 512], F32) for i in range(2)]
        psGt = PS(esD, "psGt", [128, 64], F32)
        psp1 = PS(esD, "psp1", [4, 512], F32)
        psX = PS(esD, "psX", [128, 128], BF16)
        psO0 = PS(esD, "psO0", [128, D], F32)
        psO = [psO0, psO0]

        for tg in range(4):
            c.wait("tensor", "dqg")
            if tg > 0:
                c.wait("tensor", "act")
            for dc in range(8):
                c.emit("tensor", lambda t, dc=dc, tg=tg: t.matmul(
                    psp1[:], o1T[:, dc * 4:(dc + 1) * 4],
                    xnO[:, dc * OWN + tg * 512:dc * OWN + (tg + 1) * 512],
                    start=(dc == 0), stop=(dc == 7)), inc="pe" if dc == 7 else None)
            c.wait("scalar", "pe")
            c.emit("scalar", lambda s: s.copy(out=p1sb[:], in_=psp1[:]), inc="act")
            c.wait("gpsimd", "act")
            c.emit("gpsimd", lambda g: g.dma_start(out=p1_scr, in_=p1sb[:]),
                   inc="dqo", n=16)
            c.wait("gpsimd", "dqo")
            c.emit("gpsimd", lambda g: g.dma_start(
                out=p1f[:], in_=p1_scr.rearrange("a b -> () (a b)")), inc="dqg", n=16)
            c.wait("tensor", "dqg")
            gel = {}
            w1d = {}
            for hc in range(32):
                wb_ = hc % 2
                if hc >= 2:
                    c.wait("gpsimd", "pe", w1d.get(hc - 2, 0))
                for dc in range(8):
                    c.emit("gpsimd", lambda g, hc=hc, dc=dc, wb_=wb_: g.dma_start(
                        out=w1c[wb_][:, dc * 128:(dc + 1) * 128],
                        in_=w1t[dc * 128:(dc + 1) * 128, hc * 128:(hc + 1) * 128]),
                        inc="dqg", n=16)
                c.wait("tensor", "dqg")
                if hc >= 2:
                    c.wait("tensor", "act", gel[hc - 2])
                hp = psH[wb_]
                for dc in range(8):
                    c.emit("tensor", lambda t, dc=dc, tg=tg, wb_=wb_, hp=hp: t.matmul(
                        hp[:], w1c[wb_][:, dc * 128:(dc + 1) * 128],
                        xnO[:, dc * OWN + tg * 512:dc * OWN + (tg + 1) * 512],
                        start=(dc == 0), stop=False))
                i_ = hc // 8
                c.emit("tensor", lambda t, hc=hc, i_=i_, hp=hp: t.matmul(
                    hp[:], e1sD[0:1, i_ * D + (hc % 8) * 128:i_ * D + (hc % 8) * 128 + 128],
                    p1f[0:1, i_ * 512:(i_ + 1) * 512], start=False, stop=True),
                    inc="pe")
                w1d[hc] = c.val("pe")
                c.wait("scalar", "pe", w1d[hc])
                c.emit("scalar", lambda s, hc=hc, hp=hp: s.activation(
                    out=gTg[:, hc * 512:(hc + 1) * 512], in_=hp[:], func=AF.Gelu,
                    bias=b1c_sb[:, hc:hc + 1], scale=1.0), inc="act")
                gel[hc] = c.val("act")
                c.wait("tensor", "act", gel[hc])
                if hc == 0:
                    c.emit("tensor", lambda t: t.matmul(
                        psGt[:, 0:64], zer_bf[0:1, 0:128], zer_bf[0:1, 0:64],
                        start=True, stop=False))
                for tsub in range(4):
                    c.emit("tensor", lambda t, hc=hc, tsub=tsub: t.matmul(
                        psGt[:, tsub * 16:(tsub + 1) * 16],
                        gTg[:, hc * 512 + tsub * 128:hc * 512 + (tsub + 1) * 128],
                        o2TX[:, hc * 16:(hc + 1) * 16],
                        start=False, stop=(hc == 31)),
                        inc="pe" if (hc == 31 and tsub == 3) else None)
            if tg == 0 and "gT0" in dbg:
                dump("gT0", gTg[:, 0:512])
                dump("p1d", p1f[:])
                dump("e1d", e1sD[:])
                dump("e2d", e2sD[:])
            c.wait("scalar", "pe")
            c.emit("scalar", lambda s: s.copy(out=gstok[:], in_=psGt[:]), inc="act")
            c.wait("tensor", "act")
            for tsub in range(4):
                c.emit("tensor", lambda t, tsub=tsub: t.transpose(
                    psX[0:16, 0:128], gstok[:, tsub * 16:(tsub + 1) * 16],
                    ident_bf[:]), inc="pe")
                c.wait("scalar", "pe")
                c.emit("scalar", lambda s, tsub=tsub: s.copy(
                    out=gs16[0:16, tsub * 128:(tsub + 1) * 128], in_=psX[0:16, 0:128]),
                    inc="act")
                c.wait("tensor", "act")
            if tg == 0 and "gs16d" in dbg:
                dump("gs16d", gs16[0:16, :])
            c.wait("tensor", "act")
            st_out = {}
            for tsub in range(4):
                po = psO[tsub % 2]
                c.wait("sync", "pe")
                for h in range(16):
                    c.emit("sync", lambda sy, h=h, tg=tg, tsub=tsub: sy.dma_start(
                        out=ctxg[0:64, h * 128:(h + 1) * 128],
                        in_=ctx_scr[h, :, tg * 512 + tsub * 128:
                                    tg * 512 + (tsub + 1) * 128]), inc="dql", n=16)
                ctxld = c.val("dql")
                c.wait("tensor", "dql", ctxld)
                if tsub >= 1 or tg > 0:
                    c.wait("tensor", "dve")
                for hc in range(32):
                    for piece in range(2):
                        c.emit("tensor", lambda t, hc=hc, piece=piece, tsub=tsub,
                               po=po: t.matmul(
                            po[:, piece * 512:(piece + 1) * 512],
                            gTg[:, hc * 512 + tsub * 128:hc * 512 + (tsub + 1) * 128],
                            w2tb[:, hc * D + piece * 512:hc * D + (piece + 1) * 512],
                            start=(hc == 0), stop=False))
                for h in range(16):
                    for piece in range(2):
                        c.emit("tensor", lambda t, h=h, piece=piece,
                               po=po: t.matmul(
                            po[:, piece * 512:(piece + 1) * 512],
                            ctxg[0:64, h * 128:(h + 1) * 128],
                            wo_sb[0:64, h * D + piece * 512:h * D + (piece + 1) * 512],
                            start=False, stop=False))
                for piece in range(2):
                    c.emit("tensor", lambda t, piece=piece, tsub=tsub, po=po: t.matmul(
                        po[:, piece * 512:(piece + 1) * 512],
                        gs16[0:16, tsub * 128:(tsub + 1) * 128],
                        e2sD[0:16, piece * 512:(piece + 1) * 512],
                        start=False, stop=False))
                c.wait("tensor", "dve")
                for piece in range(2):
                    c.emit("tensor", lambda t, piece=piece, po=po,
                           last=(piece == 1): t.matmul(
                        po[:, piece * 512:(piece + 1) * 512], ones1r[:],
                        brow[0:1, piece * 512:(piece + 1) * 512],
                        start=False, stop=last), inc="pe" if piece == 1 else None)
                ffd = c.val("pe")
                rb = tsub % 2
                if tsub > 0 or tg > 0:
                    c.wait("sync", "dve")
                c.emit("sync", lambda sy, tg=tg, tsub=tsub, rb=rb: sy.dma_start(
                    out=xres[rb][:], in_=x_own[tg * 512 + tsub * 128:
                                              tg * 512 + (tsub + 1) * 128, :]),
                    inc="dql", n=16)
                c.wait("vector", "dql")
                c.wait("vector", "pe", ffd)
                c.wait("vector", "dqo")
                c.emit("vector", lambda v, rb=rb, po=po: v.scalar_tensor_tensor(
                    out=outsb[rb][:], in0=po[:], scalar=1.0, in1=xres[rb][:],
                    op0=ALU.mult, op1=ALU.add), inc="dve")
                st_out[tsub] = c.val("dve")
                c.wait("sync", "dve", st_out[tsub])
                c.emit("sync", lambda sy, tg=tg, tsub=tsub, rb=rb: sy.dma_start(
                    out=out_own[tg * 512 + tsub * 128:tg * 512 + (tsub + 1) * 128, :],
                    in_=outsb[rb][:]), inc="dqo", n=16)
        esD.close()

    c.wait("sync", "dqo")
    c.barrier()

    with nc.Block() as block:
        c.finalize(block)
    es.close()
    return nc, dbg




def kernel(**inputs):
    nc, _ = build("D", debug=())
    x4 = np.ascontiguousarray(inputs["L0_attn_x"], dtype=np.float32)
    wq = np.ascontiguousarray(inputs["L0_attn_Wq"], dtype=np.float32)
    wk = np.ascontiguousarray(inputs["L0_attn_Wk"], dtype=np.float32)
    wv = np.ascontiguousarray(inputs["L0_attn_Wv"], dtype=np.float32)
    wo = np.ascontiguousarray(inputs["L0_attn_Wo"], dtype=np.float32)
    bo = np.ascontiguousarray(inputs["L0_attn_bo"], dtype=np.float32).reshape(1, D)
    wqi = np.ascontiguousarray(inputs["L0_attn_Wqi"], dtype=np.float32)
    wki = np.ascontiguousarray(inputs["L0_attn_Wki"], dtype=np.float32)
    www = np.ascontiguousarray(inputs["L0_attn_Ww"], dtype=np.float32)
    w1t = np.ascontiguousarray(np.asarray(inputs["L0_ffn_W1"], np.float32).T)
    b1c = np.ascontiguousarray(
        np.asarray(inputs["L0_ffn_b1"], np.float32).reshape(32, 128).T)
    w2t = np.ascontiguousarray(np.asarray(inputs["L0_ffn_W2"], np.float32).T)
    b2 = np.ascontiguousarray(inputs["L0_ffn_b2"], dtype=np.float32).reshape(1, D)
    scalei = np.asarray(inputs["L0_ffn_scale"], np.float32).reshape(1, 1)

    in_maps = []
    tiles_by_p = {p: list(range(p, NT, 2)) for p in (0, 1)}
    for core in range(8):
        b, p = core // 2, core % 2
        x = x4[b]
        tiles = tiles_by_p[p]
        x_own = np.ascontiguousarray(
            np.concatenate([x[t * 128:(t + 1) * 128] for t in tiles], 0))
        pos = np.zeros((128, OT), np.float32)
        for i, t in enumerate(tiles):
            pos[:, i] = t * 128 + np.arange(128)
        in_maps.append(dict(
            x_full=np.ascontiguousarray(x), x_own=x_own, pos_qt=pos,
            wq=wq, wk=wk, wv=wv, wo=wo, bo=bo, wqi=wqi, wki=wki, www=www,
            w1t=w1t, b1c=b1c, w2t=w2t, b2=b2, scalei=scalei))

    res = run_bass_kernel_spmd(nc, in_maps, core_ids=list(range(8)))
    out = np.empty((4, T, D), np.float32)
    for core in range(8):
        b, p = core // 2, core % 2
        oo = res.results[core]["out_own"]
        for i, t in enumerate(tiles_by_p[p]):
            out[b, t * 128:(t + 1) * 128] = oo[i * 128:(i + 1) * 128]
    return out
